# version 01 (final)
# baseline (speedup 1.0000x reference)
import numpy as np
import ml_dtypes
from contextlib import ExitStack

import concourse.bass as bass
import concourse.mybir as mybir
from concourse.bass_utils import run_bass_kernel_spmd

F32 = mybir.dt.float32
BF16 = mybir.dt.bfloat16
F16 = mybir.dt.float16
U8 = mybir.dt.uint8
AF = mybir.ActivationFunctionType
ALU = mybir.AluOpType
AX = mybir.AxisListType
BF = ml_dtypes.bfloat16

LN2 = 0.6931471805599453
NCORES = 8
S = 2048
D = 2048
NSEQ = 2
NH = 16
DH = 128
I0 = 5632
IM = 1024
NE = 8
TBLK = 1024
TT = 512


class Tok:
    __slots__ = ("sem", "val", "eng")

    def __init__(self, sem, val, eng):
        self.sem, self.val, self.eng = sem, val, eng


class Prog:
    ENGINES = ("sync", "tensor", "vector", "scalar", "gpsimd")

    def __init__(self, nc, stk):
        self.nc = nc
        self.stk = stk
        self.sems = {}
        self.counts = {}
        self.ops = []

    def semh(self, name):
        if name not in self.sems:
            self.sems[name] = self.stk.enter_context(self.nc.semaphore(name))
            self.counts[name] = 0
        return self.sems[name]

    def add(self, eng, fn, waits=(), sig=None, dma=False):
        tok = None
        incs = []
        flat = []
        for t in waits:
            if t is None:
                continue
            if isinstance(t, tuple):
                flat.extend(x for x in t if x is not None)
            else:
                flat.append(t)
        waits = tuple(flat)
        if sig is not None:
            self.semh(sig)
            n = 16 if dma else 1
            self.counts[sig] += n
            incs.append((sig, n))
            tok = Tok(sig, self.counts[sig], eng)
        self.ops.append((eng, fn, waits, incs))
        return tok

    def count(self, sig):
        return self.counts.get(sig, 0)

    def tok(self, sig, eng="sync"):
        self.semh(sig)
        return Tok(sig, self.counts.get(sig, 0), eng)

    def emit(self, dump=None):
        if dump:
            with open(dump, "w") as f:
                per_d = {e: [] for e in self.ENGINES}
                for i, (eng, fn, waits, incs) in enumerate(self.ops):
                    per_d[eng].append((i, waits, incs))
                for eng in self.ENGINES:
                    f.write(f"==== {eng} ====\n")
                    for (i, waits, incs) in per_d[eng]:
                        ws = " ".join(f"{t.sem}>={t.val}({t.eng})" for t in waits)
                        ins = " ".join(f"{sg}+{n}" for sg, n in incs)
                        f.write(f"op{i}: W[{ws}] I[{ins}]\n")
        max_wait = {}
        for (eng, fn, waits, incs) in self.ops:
            for t in waits:
                if t.eng == eng:
                    continue
                max_wait[t.sem] = max(max_wait.get(t.sem, 0), t.val)
        run = {}
        pruned = []
        for (eng, fn, waits, incs) in self.ops:
            new_incs = []
            for (sg, n) in incs:
                if n == 16 or run.get(sg, 0) < max_wait.get(sg, 0):
                    new_incs.append((sg, n))
                run[sg] = run.get(sg, 0) + n
            pruned.append((eng, fn, waits, new_incs))
        per = {e: [] for e in self.ENGINES}
        for (eng, fn, waits, incs) in pruned:
            per[eng].append((fn, waits, incs))
        nc = self.nc
        sems = self.sems
        with nc.Block() as block:
            def mk(eng_name):
                def emitter(eng):
                    last = {}
                    for (fn, waits, incs) in per[eng_name]:
                        need = {}
                        for t in waits:
                            if t.eng == eng_name:
                                continue
                            if last.get(t.sem, -1) >= t.val:
                                continue
                            need[t.sem] = max(need.get(t.sem, -1), t.val)
                        for sname, val in need.items():
                            eng.wait_ge(sems[sname], val)
                            last[sname] = val
                        ins = fn(eng)
                        for (s, n) in incs:
                            ins.then_inc(sems[s], n)
                return emitter

            block.sync(mk("sync"))
            block.tensor(mk("tensor"))
            block.vector(mk("vector"))
            block.scalar(mk("scalar"))
            block.gpsimd(mk("gpsimd"))


def _bytesT(packed):
    o, w = packed.shape
    p = np.ascontiguousarray(packed.T)
    b = p.view(np.uint8).reshape(w, o, 4)
    return np.ascontiguousarray(b.transpose(0, 2, 1).reshape(w * 4, o))


def _rope_tables():
    inv = 1.0 / (10000.0 ** (np.arange(0, DH, 2, dtype=np.float64) / DH))
    t = np.arange(S, dtype=np.float64)
    fr = np.outer(t, inv)
    cos = np.cos(fr).T.astype(np.float32)
    sin = np.sin(fr).T.astype(np.float32)
    cos_fm = np.concatenate([cos, cos], 0)
    sin_fm = np.concatenate([sin, sin], 0)
    return np.ascontiguousarray(cos_fm), np.ascontiguousarray(sin_fm)


def _masks():
    k = np.arange(128)[:, None]
    q = np.arange(TT)[None, :]
    m = np.stack([(128 * j + k <= q) for j in range(4)], 0)
    return np.where(m, np.float32(0), np.float32(-30000.0))


def prep_consts():
    c = {}
    c["ident_bf"] = np.eye(128, dtype=BF)
    c["ones_col_bf"] = np.ones((128, 1), dtype=BF)
    c["ones_row_bf"] = np.ones((1, TT), dtype=BF)
    c["ones1_f32"] = np.ones((1, 128), dtype=np.float32)
    sel2 = np.zeros((2, 128), dtype=np.float32)
    sel2[0, :64] = 1.0
    sel2[1, 64:] = 1.0
    c["sel2_f32"] = sel2
    esel = np.zeros((9, 9 * 128), dtype=BF)
    for e in range(9):
        esel[e, 128 * e:128 * (e + 1)] = 1.0
    c["esel_bf"] = esel
    RT = np.zeros((128, 128), dtype=np.float32)
    for dh in range(64):
        RT[dh + 64, dh] = -1.0
        RT[dh, dh + 64] = 1.0
    c["ropeR_bf"] = RT.astype(BF)
    cos_fm, sin_fm = _rope_tables()
    c["cos_fm"] = cos_fm.astype(BF)
    c["sin_fm"] = sin_fm.astype(BF)
    c["masks"] = np.ascontiguousarray(_masks()).astype(BF)
    return c


def prep_weights(inp):
    w = {}
    tr = lambda a: np.ascontiguousarray(np.asarray(a).T)
    w["gu_bytesT"] = _bytesT(np.asarray(inp["L0_ffn_gate_up_packed"]))
    w["gu_scales"] = np.ascontiguousarray(np.asarray(inp["L0_ffn_gate_up_scales"]))
    w["dn_bytesT"] = _bytesT(np.asarray(inp["L0_ffn_down_packed"]))
    w["dn_scales"] = np.ascontiguousarray(np.asarray(inp["L0_ffn_down_scales"]))
    tile4 = lambda a: np.ascontiguousarray(
        np.asarray(a).T.reshape(16, 128, 16, 128).transpose(2, 1, 0, 3))
    w["wqT"] = tile4(inp["L0_attn_Wq"])
    w["wkT"] = tile4(inp["L0_attn_Wk"])
    w["wvT"] = tile4(inp["L0_attn_Wv"])
    w["woT"] = tile4(inp["L0_attn_Wo"])
    col = lambda b: np.ascontiguousarray(np.asarray(b).reshape(16, 128).T.astype(np.float32))
    w["bq"] = col(inp["L0_attn_bq"])
    w["bk"] = col(inp["L0_attn_bk"])
    w["bv"] = col(inp["L0_attn_bv"])
    w["bo"] = col(inp["L0_attn_bo"])
    w["q1T"] = tile4(inp["L1_attn_q1_w"])
    w["k1T"] = tile4(inp["L1_attn_k1_w"])
    w["q2T"] = tile4(inp["L1_attn_q2_w"])
    w["k2T"] = tile4(inp["L1_attn_k2_w"])
    w["vT"] = tile4(inp["L1_attn_v_w"])
    w["oT"] = tile4(inp["L1_attn_o_w"])
    w["lam"] = np.asarray(inp["L1_attn_lambda_param"]).reshape(1, NH).astype(np.float32)
    w["clsT"] = tr(inp["L1_ffn_classifier_w"])
    cls_pad = np.zeros((D, 128), dtype=np.float32)
    cls_pad[:, 0:12] = w["clsT"]
    w["clsT_p"] = np.ascontiguousarray(
        cls_pad.reshape(16, 128, 128).transpose(1, 0, 2).reshape(128, 16 * 128))
    w["ebias"] = np.asarray(inp["L1_ffn_e_score_correction_bias"]).reshape(1, 12).astype(np.float32)
    w["gT"] = np.ascontiguousarray(np.asarray(inp["L1_ffn_gate_w"]).transpose(0, 2, 1))
    w["uT"] = np.ascontiguousarray(np.asarray(inp["L1_ffn_up_w"]).transpose(0, 2, 1))
    w["dT"] = np.ascontiguousarray(np.asarray(inp["L1_ffn_down_w"]).transpose(0, 2, 1))
    return w


def _emit_dequant_set(P, nc, sb, bytesT_ap, scales_ap, out_ap, n_in, n_out, tag):
    nblk = n_in // 256
    not_ = n_out // TT
    ndc = n_in // 128
    W = n_out
    nbuf = 2
    for bi in range(nblk):
        for oi in range(not_):
            it = bi * not_ + oi
            sl = it % nbuf
            bt = sb[f"{tag}_bt{sl}"]
            sc2 = sb[f"{tag}_sc{sl}"]
            ps_sc = sb[f"{tag}_pssc{sl}"]
            prev = P._deq_done.get((tag, sl))
            t_b = P.add("sync", lambda e, bt=bt, bi=bi, oi=oi: e.dma_start(
                out=bt[:], in_=bytesT_ap[128 * bi:128 * (bi + 1), TT * oi:TT * (oi + 1)]),
                waits=(prev,), sig=f"{tag}b{sl}", dma=True)
            t_s = P.add("gpsimd", lambda e, sc2=sc2, bi=bi, oi=oi: e.dma_start(
                out=sc2[:], in_=scales_ap[2 * bi:2 * bi + 2, TT * oi:TT * (oi + 1)]),
                waits=(prev,), sig=f"{tag}s{sl}", dma=True)
            t_bc = P.add("tensor", lambda e, ps_sc=ps_sc, sc2=sc2: e.matmul(
                ps_sc[:], sb["sel2"][:], sc2[:], start=True, stop=True),
                waits=(t_s, prev), sig="deq_mm")
            outs = []
            for which in (0, 1):
                n_t = sb[f"{tag}_n{sl}"]
                v_t = sb[f"{tag}_v{sl}"]
                w1 = sb[f"{tag}_w1{sl}"]
                w2 = sb[f"{tag}_w2{sl}"]
                o_t = sb[f"{tag}_o{sl}{which}"]
                t6 = sb[f"{tag}_t6{sl}{which}"]
                e2 = sb[f"{tag}_e2{sl}{which}"]
                if which == 0:
                    P.add("vector", lambda e, n_t=n_t, bt=bt: e.tensor_scalar(
                        out=n_t[:], in0=bt[:], scalar1=15, scalar2=None, op0=ALU.bitwise_and),
                        waits=(t_b,))
                else:
                    P.add("vector", lambda e, n_t=n_t, bt=bt: e.tensor_scalar(
                        out=n_t[:], in0=bt[:], scalar1=4, scalar2=None, op0=ALU.logical_shift_right),
                        waits=(t_b,))
                b8 = sb[f"{tag}_b8{sl}"]
                P.add("vector", lambda e, b8=b8, n_t=n_t: e.tensor_scalar(
                    out=b8[:], in0=n_t[:], scalar1=8, scalar2=None, op0=ALU.bitwise_and))
                sgn = sb[f"{tag}_sg{sl}"]
                P.add("vector", lambda e, sgn=sgn, b8=b8: e.tensor_scalar(
                    out=sgn[:], in0=b8[:], scalar1=-0.25, scalar2=1.0, op0=ALU.mult, op1=ALU.add))
                tv6 = P.add("vector", lambda e, t6=t6, n_t=n_t: e.tensor_scalar(
                    out=t6[:], in0=n_t[:], scalar1=6, scalar2=None, op0=ALU.bitwise_and),
                    sig="deq_v")
                ta = P.add("scalar", lambda e, e2=e2, t6=t6: e.activation(
                    out=e2[:], in_=t6[:], func=AF.Exp, scale=LN2 / 2.0),
                    waits=(tv6, P._deq_w1.get((tag, sl, which))), sig="deq_a")
                g1 = sb[f"{tag}_g1{sl}"]
                P.add("vector", lambda e, g1=g1, n_t=n_t: e.tensor_scalar(
                    out=g1[:], in0=n_t[:], scalar1=1, scalar2=None, op0=ALU.bitwise_and))
                uu = sb[f"{tag}_u{sl}"]
                P.add("vector", lambda e, uu=uu, g1=g1: e.tensor_scalar(
                    out=uu[:], in0=g1[:], scalar1=0.25, scalar2=0.5, op0=ALU.mult, op1=ALU.add))
                m7 = sb[f"{tag}_m7{sl}"]
                P.add("vector", lambda e, m7=m7, n_t=n_t: e.tensor_scalar(
                    out=m7[:], in0=n_t[:], scalar1=7, scalar2=None, op0=ALU.bitwise_and))
                dd = sb[f"{tag}_d{sl}"]
                P.add("vector", lambda e, dd=dd, m7=m7: e.tensor_scalar(
                    out=dd[:], in0=m7[:], scalar1=2.0 / 3.0, scalar2=1.0, op0=ALU.mult, op1=ALU.min))
                P._deq_w1[(tag, sl, which)] = P.add(
                    "vector", lambda e, w1=w1, uu=uu, e2=e2: e.tensor_tensor(
                        out=w1[:], in0=uu[:], in1=e2[:], op=ALU.mult),
                    waits=(ta,), sig="deq_v")
                P.add("vector", lambda e, w2=w2, w1=w1, dd=dd: e.tensor_tensor(
                    out=w2[:], in0=w1[:], in1=dd[:], op=ALU.mult))
                P.add("vector", lambda e, v_t=v_t, w2=w2, sgn=sgn: e.tensor_tensor(
                    out=v_t[:], in0=w2[:], in1=sgn[:], op=ALU.mult))
                tout = P.add("vector", lambda e, o_t=o_t, v_t=v_t, ps_sc=ps_sc: e.tensor_tensor(
                    out=o_t[:], in0=v_t[:], in1=ps_sc[:], op=ALU.mult),
                    waits=(t_bc, P._deq_store.get((tag, sl, which))), sig="deq_v")
                st_tok = None
                for ph in (0, 1):
                    dst = bass.AP(out_ap.tensor,
                                  bi * 256 + ph * 128 + which * (ndc * 128)
                                  + (4 * oi) * (128 * ndc * 128),
                                  [[2 * ndc * 128, 64],
                                   [128 * ndc * 128, 4], [1, 128]])
                    src = bass.AP(o_t, 64 * ph * TT, [[TT, 64], [128, 4], [1, 128]])
                    st_tok = P.add("sync", lambda e, src=src, dst=dst: e.dma_start(
                        out=dst, in_=src), waits=(tout,), sig=f"{tag}st", dma=True)
                P._deq_store[(tag, sl, which)] = st_tok
                outs.append(st_tok)
            P._deq_done[(tag, sl)] = outs[-1]


def build_dequant():
    nc = bass.Bass("TRN2", target_bir_lowering=False, detect_race_conditions=False)
    gu_b = nc.dram_tensor("gu_bytesT", [D // 2, 2 * I0], U8, kind="ExternalInput").ap()
    gu_s = nc.dram_tensor("gu_scales", [16, 2 * I0], F16, kind="ExternalInput").ap()
    dn_b = nc.dram_tensor("dn_bytesT", [I0 // 2, D], U8, kind="ExternalInput").ap()
    dn_s = nc.dram_tensor("dn_scales", [44, D], F16, kind="ExternalInput").ap()
    sel2_in = nc.dram_tensor("sel2", [2, 128], F32, kind="ExternalInput").ap()
    gT = nc.dram_tensor("gT", [NE, D, IM], F32, kind="ExternalInput").ap()
    uT = nc.dram_tensor("uT", [NE, D, IM], F32, kind="ExternalInput").ap()
    dT = nc.dram_tensor("dT", [NE, IM, D], F32, kind="ExternalInput").ap()
    AWN = ("q", "k", "v", "o", "q1", "k1", "q2", "k2", "v1", "o1")
    aw_in = {p: nc.dram_tensor(f"aw_{p}", [16, 128, 16, 128], F32, kind="ExternalInput").ap()
             for p in AWN}
    aw_out = {p: nc.dram_tensor(f"awb_{p}", [16, 128, 16, 128], BF16, kind="ExternalOutput").ap()
              for p in AWN}

    wgu = nc.dram_tensor("wguT_bf", [2 * I0 // 128, 128, D // 128, 128], BF16, kind="ExternalOutput").ap()
    wdn = nc.dram_tensor("wdnT_bf", [D // 128, 128, I0 // 128, 128], BF16, kind="ExternalOutput").ap()
    gTb = nc.dram_tensor("gT_bf", [NE, IM // 128, 128, D // 128, 128], BF16, kind="ExternalOutput").ap()
    uTb = nc.dram_tensor("uT_bf", [NE, IM // 128, 128, D // 128, 128], BF16, kind="ExternalOutput").ap()
    dTb = nc.dram_tensor("dT_bf", [NE, D // 128, 128, IM // 128, 128], BF16, kind="ExternalOutput").ap()

    with ExitStack() as stk:
        e = stk.enter_context
        sb = {}
        sb["sel2"] = e(nc.sbuf_tensor("sel2_sb", [2, 128], F32))
        for tag in ("gu", "dn"):
            for sl in range(2):
                sb[f"{tag}_bt{sl}"] = e(nc.sbuf_tensor(f"{tag}_bt{sl}", [128, TT], U8))
                sb[f"{tag}_sc{sl}"] = e(nc.sbuf_tensor(f"{tag}_sc{sl}", [2, TT], F32))
                sb[f"{tag}_pssc{sl}"] = e(nc.psum_tensor(f"{tag}_pssc{sl}", [128, TT], F32))
                for nm in ("n", "b8", "t6", "g1", "m7"):
                    sb[f"{tag}_{nm}{sl}"] = e(nc.sbuf_tensor(f"{tag}_{nm}{sl}", [128, TT], U8))
                for nm in ("sg", "u", "d", "w1", "w2", "v"):
                    sb[f"{tag}_{nm}{sl}"] = e(nc.sbuf_tensor(f"{tag}_{nm}{sl}", [128, TT], BF16))
                for which in (0, 1):
                    sb[f"{tag}_o{sl}{which}"] = e(nc.sbuf_tensor(f"{tag}_o{sl}{which}", [128, TT], BF16))
                    sb[f"{tag}_t6{sl}{which}"] = e(nc.sbuf_tensor(f"{tag}_t6{sl}{which}", [128, TT], U8))
                    sb[f"{tag}_e2{sl}{which}"] = e(nc.sbuf_tensor(f"{tag}_e2{sl}{which}", [128, TT], BF16))
        for sl in range(2):
            sb[f"mc{sl}"] = e(nc.sbuf_tensor(f"mc{sl}", [128, D], BF16))

        P = Prog(nc, stk)
        P._deq_done = {}
        P._deq_w1 = {}
        P._deq_store = {}
        t0 = P.add("sync", lambda e: e.dma_start(out=sb["sel2"][:], in_=sel2_in),
                   sig="cst", dma=True)
        P.add("tensor", lambda e: e.engine_nop() if hasattr(e, "engine_nop") else e.nop(),
              waits=(t0,))
        _emit_dequant_set(P, nc, sb, gu_b, gu_s, wgu, D, 2 * I0, "gu")
        _emit_dequant_set(P, nc, sb, dn_b, dn_s, wdn, I0, D, "dn")

        done = {}
        it = 0
        for (src, dst, rows) in ((gT, gTb, D), (uT, uTb, D), (dT, dTb, IM)):
            cols = src.shape[2]
            noc = cols // 128
            ndc = rows // 128
            for ei in range(NE):
                for dc in range(ndc):
                    sl = it % 2
                    buf = sb[f"mc{sl}"]
                    tl = P.add("gpsimd", lambda e, buf=buf, src=src, ei=ei, dc=dc, cols=cols:
                               e.dma_start(out=buf[:, 0:cols], in_=src[ei, 128 * dc:128 * (dc + 1), :]),
                               waits=(done.get(sl),), sig=f"mcl{sl}", dma=True)
                    dstap = bass.AP(dst.tensor,
                                    ei * (noc * 128 * ndc * 128) + dc * 128,
                                    [[ndc * 128, 128], [128 * ndc * 128, noc], [1, 128]])
                    done[sl] = P.add("sync", lambda e, buf=buf, dstap=dstap, cols=cols:
                                     e.dma_start(out=dstap, in_=buf[:, 0:cols]),
                                     waits=(tl,), sig=f"mcs{sl}", dma=True)
                    it += 1

        for p in AWN:
            for h in range(16):
                sl = it % 2
                buf = sb[f"mc{sl}"]
                tl = P.add("gpsimd", lambda e, buf=buf, p=p, h=h:
                           e.dma_start(out=buf[:].rearrange("p (c o) -> p c o", c=16),
                                       in_=aw_in[p][h]),
                           waits=(done.get(sl),), sig=f"mcl{sl}", dma=True)
                done[sl] = P.add("sync", lambda e, buf=buf, p=p, h=h:
                                 e.dma_start(out=aw_out[p][h],
                                             in_=buf[:].rearrange("p (c o) -> p c o", c=16)),
                                 waits=(tl,), sig=f"mcs{sl}", dma=True)
                it += 1

        for sig in ("gust", "dnst", "mcs0", "mcs1"):
            if P.count(sig):
                P.add("sync", lambda e: e.nop(), waits=(P.tok(sig, eng="none"),))
        P.emit()
    return nc


def dequant_inputs(w, c):
    d = dict(gu_bytesT=w["gu_bytesT"], gu_scales=w["gu_scales"],
             dn_bytesT=w["dn_bytesT"], dn_scales=w["dn_scales"],
             sel2=c["sel2_f32"], gT=w["gT"], uT=w["uT"], dT=w["dT"])
    for p, k in zip(("q", "k", "v", "o", "q1", "k1", "q2", "k2", "v1", "o1"),
                    ("wqT", "wkT", "wvT", "woT", "q1T", "k1T", "q2T", "k2T", "vT", "oT")):
        d[f"aw_{p}"] = w[k]
    return d




def _emit_ln(P, sb, banks, bank_free, src3, xn_dst, tag, stage=None):
    xf = sb[f"{tag}_xf"]
    xb = sb[f"{tag}_xb"]
    xq = sb[f"{tag}_xq"]
    if getattr(P, "_ln_alt_dest", False):
        xf = sb["xn"][:][:, 0:16 * TT * 2].bitcast(F32)
    t_x = None
    for cc in range(16):
        t_x = P.add("sync", lambda e, cc=cc, xf=xf: e.dma_start(
            out=xf[:, TT * cc:TT * (cc + 1)], in_=src3[:, cc, :]),
            waits=(P._lnw.get(tag),) if cc == 0 else (), sig=f"{tag}x", dma=True)
    if stage == "x":
        P._lnw[tag] = t_x
        return t_x
    t_xb = P.add("vector", lambda e: e.tensor_copy(out=xb[:], in_=xf[:]),
                 waits=(t_x, P._lnw.get(tag + "_xb")), sig=f"{tag}v")
    t_xq = P.add("scalar", lambda e: e.activation(out=xq[:], in_=xf[:], func=AF.Square),
                 waits=(t_x, P._lnw.get(tag + "_xq")), sig=f"{tag}a")
    t_s = t_q = None
    for c in range(16):
        w0 = (t_xb, bank_free.get("ln_sum")) if c == 0 else ()
        t_s = P.add("tensor", lambda e, c=c: e.matmul(
            banks["sum"], sb["ones_col"][:], xb[:, TT * c:TT * (c + 1)],
            start=(c == 0), stop=(c == 15)), waits=w0, sig=f"{tag}mm" if c == 15 else None)
        w1 = (t_xq, bank_free.get("ln_sq")) if c == 0 else ()
        t_q = P.add("tensor", lambda e, c=c: e.matmul(
            banks["sq"], sb["ones_col"][:], xq[:, TT * c:TT * (c + 1)],
            start=(c == 0), stop=(c == 15)), waits=w1, sig=f"{tag}mm" if c == 15 else None)
    if stage == "xb":
        P._lnw[tag] = t_xq
        P._lnw[tag + "_xb"] = t_xb
        P._lnw[tag + "_xq"] = t_xq
        return t_xq
    if stage == "stats":
        P._lnw[tag] = t_q
        P._lnw[tag + "_xb"] = t_s
        P._lnw[tag + "_xq"] = t_q
        bank_free["ln_sum"] = Tok(t_s.sem, t_s.val, "none")
        bank_free["ln_sq"] = Tok(t_q.sem, t_q.val, "none")
        return t_q
    m_row, t2_row, v_row, s_row, r_row = (sb[f"{tag}_m"], sb[f"{tag}_t2"], sb[f"{tag}_v"],
                                          sb[f"{tag}_s"], sb[f"{tag}_r"])
    P.add("vector", lambda e: e.tensor_scalar(out=m_row[:], in0=banks["sum"], scalar1=1.0 / D,
                                              scalar2=None, op0=ALU.mult), waits=(t_s, t_q))
    P.add("vector", lambda e: e.tensor_tensor(out=t2_row[:], in0=m_row[:], in1=m_row[:], op=ALU.mult))
    P.add("vector", lambda e: e.tensor_scalar(out=t2_row[:], in0=t2_row[:], scalar1=1e-5,
                                              scalar2=None, op0=ALU.subtract))
    sq_row = sb[f"{tag}_sqr"]
    P.add("vector", lambda e: e.tensor_scalar(
        out=sq_row[:], in0=banks["sq"], scalar1=1.0 / D, scalar2=None, op0=ALU.mult))
    t_v = P.add("vector", lambda e: e.tensor_tensor(
        out=v_row[:], in0=sq_row[:], in1=t2_row[:], op=ALU.subtract), sig=f"{tag}v")
    bank_free["ln_sum"] = t_v
    bank_free["ln_sq"] = t_v
    t_sq = P.add("scalar", lambda e: e.activation(out=s_row[:], in_=v_row[:], func=AF.Ln),
                 waits=(t_v,), sig=f"{tag}a")
    t_r = P.add("scalar", lambda e: e.activation(out=r_row[:], in_=s_row[:], func=AF.Exp, scale=-0.5),
                sig=f"{tag}a")
    t_mb = P.add("tensor", lambda e: e.matmul(banks["mb"], sb["ones1_f32"][:], m_row[:],
                                              start=True, stop=True),
                 waits=(t_v, bank_free.get("ln_mb")), sig=f"{tag}mm")
    t_rb = P.add("tensor", lambda e: e.matmul(banks["rb"], sb["ones1_f32"][:], r_row[:],
                                              start=True, stop=True),
                 waits=(t_r, bank_free.get("ln_rb")), sig=f"{tag}mm")
    P._lnw[tag + "_xb"] = t_s
    P._lnw[tag + "_xq"] = t_q
    tmp = sb[f"{tag}_tmp"]
    t_xn = None
    for c in range(16):
        P.add("vector", lambda e, c=c: e.tensor_tensor(
            out=tmp[:], in0=xf[:, TT * c:TT * (c + 1)], in1=banks["mb"], op=ALU.subtract),
            waits=(t_mb,))
        t_xn = P.add("vector", lambda e, c=c: e.tensor_tensor(
            out=xn_dst(c), in0=tmp[:], in1=banks["rb"], op=ALU.mult),
            waits=(t_rb, P._xn_free), sig=f"{tag}v")
    bank_free["ln_mb"] = t_xn
    bank_free["ln_rb"] = t_xn
    P._lnw[tag] = t_xn
    return t_xn


def build_ffn0(nseq=NSEQ, nblk=S // TBLK, upto="full"):
    nc = bass.Bass("TRN2", target_bir_lowering=False, detect_race_conditions=False)
    x_in = nc.dram_tensor("x_in", [nseq, D, nblk * TBLK], F32, kind="ExternalInput").ap()
    wgu = nc.dram_tensor("wguT_bf", [2 * I0 // 128, 128, D // 128, 128], BF16, kind="ExternalInput").ap()
    wdn = nc.dram_tensor("wdnT_bf", [D // 128, 128, I0 // 128, 128], BF16, kind="ExternalInput").ap()
    ones_col_in = nc.dram_tensor("ones_col", [128, 1], BF16, kind="ExternalInput").ap()
    ones1_in = nc.dram_tensor("ones1", [1, 128], F32, kind="ExternalInput").ap()
    x_out = nc.dram_tensor("x_out", [nseq, D, nblk * TBLK], F32, kind="ExternalOutput").ap()
    dbg = None
    if upto != "full":
        dbg = nc.dram_tensor("dbg", [128, 48 * TBLK], BF16, kind="ExternalOutput").ap()

    NIC = I0 // 128
    with ExitStack() as stk:
        e = stk.enter_context
        sb = {}
        sb["ones_col"] = e(nc.sbuf_tensor("ones_col_sb", [128, 1], BF16))
        sb["ones1_f32"] = e(nc.sbuf_tensor("ones1_sb", [1, 128], F32))
        sb["ident"] = e(nc.sbuf_tensor("ident_sb", [128, 128], BF16))
        for sl in range(2):
            sb[f"vts{sl}"] = e(nc.sbuf_tensor(f"vts{sl}", [128, 128], BF16))
        sb["ln_tmp"] = e(nc.sbuf_tensor("ln_tmp", [128, TT], F32))
        for nm in ("ln_m", "ln_t2", "ln_v", "ln_s", "ln_r", "ln_sqr"):
            sb[nm] = e(nc.sbuf_tensor(nm, [1, TT], F32))
        sb["xn"] = e(nc.sbuf_tensor("xn", [128, 16 * TBLK], BF16))
        sb["h"] = e(nc.sbuf_tensor("h", [128, NIC * TBLK], BF16))
        sb["ln_xf"] = sb["h"][:][:, 0:2 * 16 * TT].bitcast(F32)
        sb["ln_xb"] = sb["h"][:][:, 2 * 16 * TT:3 * 16 * TT]
        sb["ln_xq"] = sb["h"][:][:, 3 * 16 * TT:4 * 16 * TT]
        sb["y"] = sb["xn"]
        for sl in range(2):
            sb[f"wg{sl}"] = e(nc.sbuf_tensor(f"wg{sl}", [128, 16 * 128], BF16))
            sb[f"wu{sl}"] = e(nc.sbuf_tensor(f"wu{sl}", [128, 16 * 128], BF16))
            sb[f"wd{sl}"] = e(nc.sbuf_tensor(f"wd{sl}", [128, NIC * 128], BF16))
            sb[f"sg{sl}"] = e(nc.sbuf_tensor(f"sg{sl}", [128, TT], BF16))
            sb[f"ysq{sl}"] = e(nc.sbuf_tensor(f"ysq{sl}", [128, TT], BF16))
            sb[f"fo{sl}"] = e(nc.sbuf_tensor(f"fo{sl}", [128, TT], F32))
            sb[f"ft{sl}"] = e(nc.sbuf_tensor(f"ft{sl}", [128, TT], F32))
        for sl in range(4):
            sb[f"xr{sl}"] = e(nc.sbuf_tensor(f"xr{sl}", [128, TT], F32))
        for nm in ("rms_m0", "rms_m1", "rms_s0", "rms_s1", "rms_r0", "rms_r1"):
            sb[nm] = e(nc.sbuf_tensor(nm, [1, TT], F32))
        bank = [e(nc.psum_tensor(f"bank{i}", [128, TT], F32)) for i in range(8)]

        P = Prog(nc, stk)
        P._lnw = {}
        P._xn_free = None
        P._h_free = None
        bank_free = {}
        t_c0 = P.add("sync", lambda e: e.dma_start(out=sb["ones_col"][:], in_=ones_col_in), sig="cst", dma=True)
        t_c1 = P.add("sync", lambda e: e.dma_start(out=sb["ones1_f32"][:], in_=ones1_in), sig="cst", dma=True)
        P.add("tensor", lambda e: e.nop(), waits=(t_c0, t_c1))

        lnbanks = {"sum": bank[0][0:1, :], "sq": bank[1][0:1, :], "mb": bank[2][:], "rb": bank[3][:]}
        wslot_free = {}
        psum_free = {}
        for seq in range(nseq):
            if upto != "full" and seq > 0:
                break
            for bi in range(nblk):
                t0 = bi * TBLK
                t_xn = None
                for tt in range(2):
                    src3 = x_in[seq].rearrange("(c p) t -> p c t", p=128)[:, :, t0 + TT * tt: t0 + TT * (tt + 1)]
                    xn_dst = lambda c, tt=tt: sb["xn"][:, TBLK * c + TT * tt: TBLK * c + TT * (tt + 1)]
                    t_xn = _emit_ln(P, sb, lnbanks, bank_free, src3, xn_dst, "ln")
                if upto == "ln":
                    t_d = P.add("sync", lambda e: e.dma_start(out=dbg[:, 0:16 * TBLK], in_=sb["xn"][:]),
                                waits=(t_xn,), sig="st", dma=True)
                    break
                t_h = None
                for ic in range(NIC):
                    sl = ic % 2
                    wgb, wub = sb[f"wg{sl}"], sb[f"wu{sl}"]
                    t_wg = P.add("sync", lambda e, wgb=wgb, ic=ic: e.dma_start(
                        out=wgb[:].rearrange("p (c o) -> p c o", c=16), in_=wgu[ic]),
                        waits=(wslot_free.get(("g", sl)),), sig=f"wg{sl}", dma=True)
                    t_wu = P.add("sync", lambda e, wub=wub, ic=ic: e.dma_start(
                        out=wub[:].rearrange("p (c o) -> p c o", c=16), in_=wgu[NIC + ic]),
                        waits=(wslot_free.get(("u", sl)),), sig=f"wu{sl}", dma=True)
                    for tt in range(2):
                        gb = bank[4 + tt][:]
                        ub = bank[6 + tt][:]
                        t_gmm = t_umm = None
                        for c in range(16):
                            w = (t_wg, t_xn, psum_free.get(("g", tt))) if c == 0 else ()
                            t_gmm = P.add("tensor", lambda e, c=c, gb=gb, wgb=wgb, tt=tt: e.matmul(
                                gb, wgb[:, 128 * c:128 * (c + 1)],
                                sb["xn"][:, TBLK * c + TT * tt: TBLK * c + TT * (tt + 1)],
                                start=(c == 0), stop=(c == 15)),
                                waits=w, sig="gmm" if c == 15 else None)
                        for c in range(16):
                            w = (t_wu, t_xn, psum_free.get(("u", tt))) if c == 0 else ()
                            t_umm = P.add("tensor", lambda e, c=c, ub=ub, wub=wub, tt=tt: e.matmul(
                                ub, wub[:, 128 * c:128 * (c + 1)],
                                sb["xn"][:, TBLK * c + TT * tt: TBLK * c + TT * (tt + 1)],
                                start=(c == 0), stop=(c == 15)),
                                waits=w, sig="umm" if c == 15 else None)
                        if tt == 1:
                            wslot_free[("g", sl)] = t_gmm
                            wslot_free[("u", sl)] = t_umm
                        esl = tt
                        sgb = sb[f"sg{esl}"]
                        t_sg = P.add("scalar", lambda e, sgb=sgb, gb=gb: e.activation(
                            out=sgb[:], in_=gb, func=AF.Silu), waits=(t_gmm, psum_free.get(("sg", esl)),), sig="sga")
                        psum_free[("g", tt)] = t_sg
                        t_h = P.add("vector", lambda e, sgb=sgb, ub=ub, ic=ic, tt=tt: e.tensor_tensor(
                            out=sb["h"][:, TBLK * ic + TT * tt: TBLK * ic + TT * (tt + 1)],
                            in0=sgb[:], in1=ub, op=ALU.mult),
                            waits=(t_umm, t_sg, P._h_free), sig="hout")
                        psum_free[("u", tt)] = t_h
                        psum_free[("sg", esl)] = t_h
                P._xn_free = (t_gmm, t_umm)
                t_hall = P.tok("hout", eng="vector")
                if upto == "gateup":
                    t_d = P.add("sync", lambda e: e.dma_start(out=dbg[:, 0:NIC * TBLK], in_=sb["h"][:]),
                                waits=(t_hall,), sig="st", dma=True)
                    break
                s2b = [bank[2][0:1, :], bank[3][0:1, :]]
                t_s2 = [None, None]
                t_yev = {}
                for o2 in range(16):
                    sl = o2 % 2
                    wdb = sb[f"wd{sl}"]
                    t_wd = P.add("sync", lambda e, wdb=wdb, o2=o2: e.dma_start(
                        out=wdb[:].rearrange("p (c o) -> p c o", c=NIC), in_=wdn[o2]),
                        waits=(wslot_free.get(("d", sl)),), sig=f"wd{sl}", dma=True)
                    for tt in range(2):
                        psl = (o2 * 2 + tt) % 2
                        yb = bank[psl][:]
                        t_ymm = None
                        for c in range(NIC):
                            w = (t_wd, t_hall, psum_free.get(("y", psl)), psum_free.get(("y2", psl)),
                                 bank_free.get("ln_sum") if psl == 0 else bank_free.get("ln_sq")) if c == 0 else ()
                            t_ymm = P.add("tensor", lambda e, c=c, yb=yb, wdb=wdb, tt=tt: e.matmul(
                                yb, wdb[:, 128 * c:128 * (c + 1)],
                                sb["h"][:, TBLK * c + TT * tt: TBLK * c + TT * (tt + 1)],
                                start=(c == 0), stop=(c == NIC - 1)),
                                waits=w, sig="ymm" if c == NIC - 1 else None)
                        if tt == 1:
                            wslot_free[("d", sl)] = t_ymm
                        t_ymm_last = t_ymm
                        t_ycp = P.add("vector", lambda e, yb=yb, o2=o2, tt=tt: e.tensor_copy(
                            out=sb["y"][:, TBLK * o2 + TT * tt: TBLK * o2 + TT * (tt + 1)], in_=yb),
                            waits=(t_ymm,), sig="yev")
                        psum_free[("y", psl)] = t_ycp
                P._h_free = t_ymm_last
                bank_free["ln_sum"] = psum_free[("y", 0)]
                bank_free["ln_sq"] = psum_free[("y", 1)]
                t_yall = P.tok("yev", eng="vector")
                for tt in range(2):
                    for o2 in range(16):
                        psl = o2 % 2
                        ysb = sb[f"ysq{psl}"]
                        t_ysq = P.add("scalar", lambda e, ysb=ysb, o2=o2, tt=tt: e.activation(
                            out=ysb[:], in_=sb["y"][:, TBLK * o2 + TT * tt: TBLK * o2 + TT * (tt + 1)],
                            func=AF.Square),
                            waits=(t_yall, psum_free.get(("ysq", psl)),), sig="sga")
                        t_s2[tt] = P.add("tensor", lambda e, ysb=ysb, tt=tt, o2=o2, s2b=s2b: e.matmul(
                            s2b[tt], sb["ones_col"][:], ysb[:],
                            start=(o2 == 0), stop=(o2 == 15)),
                            waits=(t_ysq,) + (() if o2 else (
                                bank_free.get("s2_0" if tt == 0 else "s2_1"),
                                bank_free.get("ln_mb" if tt == 0 else "ln_rb"),)),
                            sig="s2mm")
                        psum_free[("ysq", psl)] = t_s2[tt]
                if upto in ("down", "down_nostats"):
                    t_d = P.add("sync", lambda e: e.dma_start(out=dbg[:, 0:16 * TBLK], in_=sb["y"][:]),
                                waits=(psum_free[("y", 0)], psum_free[("y", 1)]), sig="st", dma=True)
                    break
                t_rb2 = [None, None]
                for tt in range(2):
                    m2, s2r, r2 = sb[f"rms_m{tt}"], sb[f"rms_s{tt}"], sb[f"rms_r{tt}"]
                    t_m2 = P.add("vector", lambda e, m2=m2, tt=tt, s2b=s2b: e.tensor_scalar(
                        out=m2[:], in0=s2b[tt], scalar1=1.0 / D, scalar2=1e-6,
                        op0=ALU.mult, op1=ALU.add),
                        waits=(t_s2[tt],), sig="vv")
                    t_sq2 = P.add("scalar", lambda e, s2r=s2r, m2=m2: e.activation(
                        out=s2r[:], in_=m2[:], func=AF.Ln), waits=(t_m2,), sig="sga")
                    t_rc = P.add("scalar", lambda e, r2=r2, s2r=s2r: e.activation(
                        out=r2[:], in_=s2r[:], func=AF.Exp, scale=-0.5), sig="sga")
                    t_r5 = P.add("vector", lambda e, r2=r2: e.tensor_scalar(
                        out=r2[:], in0=r2[:], scalar1=0.5, scalar2=None, op0=ALU.mult),
                        waits=(t_rc,), sig="vv")
                    bank_free[f"s2_{tt}"] = t_r5
                    if tt == 0:
                        bank_free["ln_mb"] = t_r5
                    else:
                        bank_free["ln_rb"] = t_r5
                    t_rb2[tt] = P.add("tensor", lambda e, r2=r2, tt=tt: e.matmul(
                        bank[4 + tt][:], sb["ones1_f32"][:], r2[:], start=True, stop=True),
                        waits=(t_r5, psum_free.get(("g", tt)), psum_free.get(("u", 0)), psum_free.get(("u", 1)),
                               bank_free.get(f"rb2_{tt}")), sig="gmm")
                t_last = None
                for o2 in range(16):
                    for tt in range(2):
                        it = o2 * 2 + tt
                        xsl = it % 4
                        osl = it % 2
                        xrb = sb[f"xr{xsl}"]
                        src = x_in[seq].rearrange("(c p) t -> p c t", p=128)[:, o2, t0 + TT * tt: t0 + TT * (tt + 1)]
                        t_xr = P.add("sync", lambda e, xrb=xrb, src=src: e.dma_start(out=xrb[:], in_=src),
                                     waits=(bank_free.get(f"xr{xsl}"),), sig=f"xr{xsl}", dma=True)
                        ftb, fob = sb[f"ft{osl}"], sb[f"fo{osl}"]
                        P.add("vector", lambda e, ftb=ftb, o2=o2, tt=tt: e.tensor_tensor(
                            out=ftb[:], in0=sb["y"][:, TBLK * o2 + TT * tt: TBLK * o2 + TT * (tt + 1)],
                            in1=bank[4 + tt][:], op=ALU.mult),
                            waits=(t_rb2[tt], bank_free.get(f"ft{osl}"),))
                        t_fo = P.add("vector", lambda e, fob=fob, ftb=ftb, xrb=xrb: e.tensor_tensor(
                            out=fob[:], in0=ftb[:], in1=xrb[:], op=ALU.add),
                            waits=(t_xr, bank_free.get(f"fo{osl}"),), sig="vv")
                        bank_free[f"ft{osl}"] = t_fo
                        bank_free[f"xr{xsl}"] = t_fo
                        dst = x_out[seq].rearrange("(c p) t -> p c t", p=128)[:, o2, t0 + TT * tt: t0 + TT * (tt + 1)]
                        t_st = P.add("sync", lambda e, fob=fob, dst=dst: e.dma_start(out=dst, in_=fob[:]),
                                     waits=(t_fo,), sig="st", dma=True)
                        bank_free[f"fo{osl}"] = t_st
                        t_last = t_st
                        bank_free[f"rb2_{tt}"] = t_fo
                psum_free[("g", 0)] = bank_free["rb2_0"]
                psum_free[("g", 1)] = bank_free["rb2_1"]
        P.add("sync", lambda e: e.nop(), waits=(Tok("st", P.counts["st"], "none"),))
        P.emit()
    return nc


def ffn0_inputs(x_fm, deq, c):
    return dict(x_in=x_fm, wguT_bf=deq["wguT_bf"], wdnT_bf=deq["wdnT_bf"],
                ones_col=c["ones_col_bf"], ones1=c["ones1_f32"])


def _emit_rope(P, sb, bank, pb, peb, esl, tt, t_mm, t_war, bcol, rot_free):
    qv = sb[f"qv{esl}"]
    tmp = sb["rope_tmp"]
    tmp2 = sb["rope_s1"]
    cosv = sb["cos"][:, TT * tt:TT * (tt + 1)]
    sinv = sb["sin"][:, TT * tt:TT * (tt + 1)]
    rot = bank[6 + esl]
    t_qv = P.add("vector", lambda e: e.tensor_scalar(
        out=qv[:], in0=pb[:], scalar1=bcol, scalar2=None, op0=ALU.add),
        waits=(t_mm, rot_free.get(("qv", esl))), sig="pev")
    t_rot = P.add("tensor", lambda e: e.matmul(rot[:], sb["ropeR"][:], qv[:],
                                               start=True, stop=True),
                  waits=(t_qv, rot_free.get(("rot", esl))), sig="rotm")
    rot_free[("qv", esl)] = t_rot
    P.add("vector", lambda e: e.tensor_tensor(out=tmp[:], in0=qv[:], in1=cosv, op=ALU.mult))
    t_b = P.add("vector", lambda e: e.tensor_tensor(out=tmp2[:], in0=rot[:], in1=sinv, op=ALU.mult),
                waits=(t_rot,), sig="pev")
    rot_free[("rot", esl)] = t_b
    return P.add("vector", lambda e: e.tensor_tensor(
        out=peb[:], in0=tmp[:], in1=tmp2[:], op=ALU.add), waits=(t_war,), sig="pev")


def build_attn(layer, nseq=NSEQ, upto="full"):
    nc = bass.Bass("TRN2", target_bir_lowering=False, detect_race_conditions=False,
                   dynamic_dma_scratch_size=2048)
    x_in = nc.dram_tensor("x_in", [nseq, D, S], F32, kind="ExternalInput").ap()
    projs = ["q", "k", "v"] if layer == 0 else ["q1", "k1", "q2", "k2", "v"]
    wts = {p: nc.dram_tensor(f"w_{p}", [16, 128, 16, 128], BF16, kind="ExternalInput").ap() for p in projs}
    wo = nc.dram_tensor("w_o", [16, 128, 16, 128], BF16, kind="ExternalInput").ap()
    ones_col_in = nc.dram_tensor("ones_col", [128, 1], BF16, kind="ExternalInput").ap()
    ones_row_in = nc.dram_tensor("ones_row", [1, TT], BF16, kind="ExternalInput").ap()
    ones1_in = nc.dram_tensor("ones1", [1, 128], F32, kind="ExternalInput").ap()
    ident_in = nc.dram_tensor("ident", [128, 128], BF16, kind="ExternalInput").ap()
    if layer == 0:
        bias_in = {p: nc.dram_tensor(f"b_{p}", [128, 16], F32, kind="ExternalInput").ap() for p in projs}
        bo_in = nc.dram_tensor("b_o", [128, 16], F32, kind="ExternalInput").ap()
        cos_in = nc.dram_tensor("cos_fm", [128, S], BF16, kind="ExternalInput").ap()
        ropeR_in = nc.dram_tensor("ropeR", [128, 128], BF16, kind="ExternalInput").ap()
        sin_in = nc.dram_tensor("sin_fm", [128, S], BF16, kind="ExternalInput").ap()
        mask_in = nc.dram_tensor("masks", [4, 128, TT], BF16, kind="ExternalInput").ap()
    else:
        lam_in = nc.dram_tensor("lam", [1, NH], F32, kind="ExternalInput").ap()
    x_out = nc.dram_tensor("x_out", [nseq, D, S], F32, kind="ExternalOutput").ap()
    dbg = None
    if upto != "full":
        dbg = nc.dram_tensor("dbg", [128, 16 * S], BF16, kind="ExternalOutput").ap()
    scr = {p: nc.dram_tensor(f"scr_{p}", [NH, 128, S], BF16, kind="Internal").ap() for p in projs}

    NQT = S // TT
    nset = 1 if layer == 0 else 2
    with ExitStack() as stk:
        e = stk.enter_context
        sb = {}
        sb["ones_col"] = e(nc.sbuf_tensor("ones_col_sb", [128, 1], BF16))
        sb["ones_row"] = e(nc.sbuf_tensor("ones_row_sb", [1, TT], BF16))
        sb["ones1_f32"] = e(nc.sbuf_tensor("ones1_sb", [1, 128], F32))
        sb["ident"] = e(nc.sbuf_tensor("ident_sb", [128, 128], BF16))
        for sl in range(2):
            sb[f"vts{sl}"] = e(nc.sbuf_tensor(f"vts{sl}", [128, 128], BF16))
        sb["ln_tmp"] = e(nc.sbuf_tensor("ln_tmp", [128, TT], F32))
        for nm in ("ln_m", "ln_t2", "ln_v", "ln_s", "ln_r", "ln_sqr"):
            sb[nm] = e(nc.sbuf_tensor(nm, [1, TT], F32))
        bufA = e(nc.sbuf_tensor("bufA", [128, 16 * S], BF16))
        bufB = e(nc.sbuf_tensor("bufB", [128, 16 * S], BF16))
        sb["xn"] = bufA
        sb["ln_xf"] = bufB[:][:, 0:2 * 16 * TT].bitcast(F32)
        sb["ln_xb"] = bufB[:][:, 2 * 16 * TT:3 * 16 * TT]
        sb["ln_xq"] = bufB[:][:, 3 * 16 * TT:4 * 16 * TT]
        ctx_all = bufB
        if layer == 0:
            sb["cos"] = e(nc.sbuf_tensor("cos_sb", [128, S], BF16))
            sb["sin"] = e(nc.sbuf_tensor("sin_sb", [128, S], BF16))
            sb["mask"] = e(nc.sbuf_tensor("mask_sb", [128, 4 * TT], BF16))
            sb["bias"] = e(nc.sbuf_tensor("bias_sb", [128, (len(projs) + 1) * 16], F32))
            sb["rope_tmp"] = e(nc.sbuf_tensor("rope_tmp", [128, TT], F32))
            sb["rope_s1"] = sb["ln_tmp"]
            sb["ropeR"] = e(nc.sbuf_tensor("ropeR_sb", [128, 128], BF16))
            for sl in range(2):
                sb[f"qv{sl}"] = e(nc.sbuf_tensor(f"qv{sl}", [128, TT], BF16))
        else:
            sb["lam"] = e(nc.sbuf_tensor("lam_sb", [1, NH], F32))
        for sl in range(2):
            sb[f"wt{sl}"] = e(nc.sbuf_tensor(f"wt{sl}", [128, 16 * 128], BF16))
            sb[f"pe{sl}"] = e(nc.sbuf_tensor(f"pe{sl}", [128, TT], BF16))
            for p in projs:
                sb[f"hd_{p}{sl}"] = e(nc.sbuf_tensor(f"hd_{p}{sl}", [128, S], BF16))
            sb[f"xr{sl}"] = e(nc.sbuf_tensor(f"xr{sl}", [128, TT], F32))
            sb[f"fo{sl}"] = e(nc.sbuf_tensor(f"fo{sl}", [128, TT], F32))
            sb[f"cb{sl}"] = e(nc.sbuf_tensor(f"cb{sl}", [128, TT], BF16))
            sb[f"tb{sl}"] = e(nc.sbuf_tensor(f"tb{sl}", [128, TT], BF16))
        for st_ in range(nset):
            for sl in range(2):
                sb[f"eb{st_}{sl}"] = e(nc.sbuf_tensor(f"eb{st_}{sl}", [128, TT], BF16))
        sb["r1row"] = e(nc.sbuf_tensor("r1row", [1, TT], F32))
        sb["r2row"] = e(nc.sbuf_tensor("r2row", [1, TT], F32))
        bank = [e(nc.psum_tensor(f"bank{i}", [128, TT], F32)) for i in range(8)]

        P = Prog(nc, stk)
        P._lnw = {}
        P._xn_free = None
        P._ctx_war = None
        P._rot_free = {}
        P._vts = {}
        bank_free = {}
        psum_free = {}
        wslot_free = {}
        head_free = {}
        scr_free = [None]
        lnbanks = {"sum": bank[0][0:1, :], "sq": bank[1][0:1, :], "mb": bank[2][:], "rb": bank[3][:]}

        tc = []
        tc.append(P.add("sync", lambda e: e.dma_start(out=sb["ones_col"][:], in_=ones_col_in), sig="cst", dma=True))
        tc.append(P.add("sync", lambda e: e.dma_start(out=sb["ones_row"][:], in_=ones_row_in), sig="cst", dma=True))
        tc.append(P.add("sync", lambda e: e.dma_start(out=sb["ones1_f32"][:], in_=ones1_in), sig="cst", dma=True))
        tc.append(P.add("sync", lambda e: e.dma_start(out=sb["ident"][:], in_=ident_in), sig="cst", dma=True))
        if layer == 0:
            tc.append(P.add("sync", lambda e: e.dma_start(out=sb["cos"][:], in_=cos_in), sig="cst", dma=True))
            tc.append(P.add("sync", lambda e: e.dma_start(out=sb["sin"][:], in_=sin_in), sig="cst", dma=True))
            tc.append(P.add("sync", lambda e: e.dma_start(out=sb["ropeR"][:], in_=ropeR_in), sig="cst", dma=True))
            tc.append(P.add("sync", lambda e: e.dma_start(
                out=sb["mask"][:].rearrange("p (j t) -> p j t", j=4),
                in_=mask_in.rearrange("j p t -> p j t")), sig="cst", dma=True))
            for i, p in enumerate(projs):
                tc.append(P.add("sync", lambda e, p=p, i=i: e.dma_start(
                    out=sb["bias"][:, 16 * i:16 * (i + 1)], in_=bias_in[p]), sig="cst", dma=True))
            tc.append(P.add("sync", lambda e: e.dma_start(
                out=sb["bias"][:, 16 * len(projs):16 * (len(projs) + 1)], in_=bo_in), sig="cst", dma=True))
        else:
            tc.append(P.add("sync", lambda e: e.dma_start(out=sb["lam"][:], in_=lam_in), sig="cst", dma=True))
        t_const = tuple(tc)

        for seq in range(nseq):
            if upto in ("ln", "proj", "attnc") and seq > 0:
                break
            if seq == 1 and upto == "s1z":
                P.add("sync", lambda e: e.dma_start(out=dbg, in_=sb["xn"][:]),
                      waits=(Tok("st", P.counts["st"], "none"),), sig="st", dma=True)
                break
            if seq > 0 and upto in ("s1barrier", "s1abar"):
                tbar = Tok("st", P.counts["st"], "none")
                for eng in ("sync", "tensor", "vector", "scalar", "gpsimd"):
                    P.add(eng, lambda e: e.nop(), waits=(tbar,))
            if seq == 1 and upto == "s1m6":
                tbar = Tok("st", P.counts["st"], "none")
                P.add("sync", lambda e: e.nop(), waits=(tbar,))
                xf6 = sb["ln_xf"]
                src6 = x_in[1].rearrange("(c p) t -> p c t", p=128)[:, :, 0:TT]
                t_x6 = None
                for cc in range(16):
                    t_x6 = P.add("sync", lambda e, cc=cc, xf6=xf6, src6=src6: e.dma_start(
                        out=xf6[:, TT * cc:TT * (cc + 1)], in_=src6[:, cc, :]), sig="lnx", dma=True)
                P.add("sync", lambda e, xf6=xf6: e.dma_start(
                    out=dbg[:, 0:2 * S].bitcast(F32), in_=xf6[:, 0:S]),
                    waits=(t_x6,), sig="st", dma=True)
                break
            if seq == 1 and upto in ("s1m2", "s1m3", "s1m4"):
                tbar = Tok("st", P.counts["st"], "none")
                xf2 = sb["ln_xf"]
                t_v1 = P.add("vector", lambda e, xf2=xf2: e.tensor_copy(
                    out=sb["ln_xb"][:], in_=xf2[:, 0:16 * TT]), waits=(tbar,), sig="lnv")
                t_q1 = P.add("scalar", lambda e, xf2=xf2: e.activation(
                    out=sb["ln_xq"][:], in_=xf2[:, 0:16 * TT], func=AF.Square),
                    waits=(tbar,), sig="lna")
                t_s = t_q = None
                for c in range(16):
                    t_s = P.add("tensor", lambda e, c=c: e.matmul(
                        lnbanks["sum"], sb["ones_col"][:], sb["ln_xb"][:, TT * c:TT * (c + 1)],
                        start=(c == 0), stop=(c == 15)),
                        waits=(t_v1, tbar) if c == 0 else (), sig="lnmm" if c == 15 else None)
                    t_q = P.add("tensor", lambda e, c=c: e.matmul(
                        lnbanks["sq"], sb["ones_col"][:], sb["ln_xq"][:, TT * c:TT * (c + 1)],
                        start=(c == 0), stop=(c == 15)),
                        waits=(t_q1, tbar) if c == 0 else (), sig="lnmm" if c == 15 else None)
                t_m2 = P.add("vector", lambda e: e.tensor_scalar(
                    out=sb["ln_m"][:], in0=lnbanks["sum"], scalar1=1.0 / D, scalar2=None,
                    op0=ALU.mult), waits=(t_s, t_q), sig="lnv")
                tw = [t_m2]
                if upto == "s1m4":
                    t_sq = P.add("scalar", lambda e: e.activation(
                        out=sb["ln_s"][:], in_=sb["ln_m"][:], func=AF.Sqrt),
                        waits=(t_m2,), sig="lna")
                    tw.append(t_sq)
                if upto == "s1m3":
                    t_r1 = P.add("vector", lambda e: e.reciprocal(out=sb["ln_r"][:], in_=sb["ln_m"][:]), sig="lnv")
                    t_mb = P.add("tensor", lambda e: e.matmul(lnbanks["mb"], sb["ones1_f32"][:], sb["ln_m"][:],
                                                              start=True, stop=True),
                                 waits=(t_m2, bank_free.get("ln_mb")), sig="lnmm")
                    t_rb = P.add("tensor", lambda e: e.matmul(lnbanks["rb"], sb["ones1_f32"][:], sb["ln_r"][:],
                                                              start=True, stop=True),
                                 waits=(t_r1, bank_free.get("ln_rb")), sig="lnmm")
                    for c in range(4):
                        P.add("vector", lambda e, c=c: e.tensor_tensor(
                            out=sb["ln_tmp"][:], in0=xf2[:, TT * c:TT * (c + 1)], in1=lnbanks["mb"],
                            op=ALU.subtract), waits=(t_mb,))
                        tw.append(P.add("vector", lambda e, c=c: e.tensor_tensor(
                            out=sb["xn"][:, TT * c:TT * (c + 1)], in0=sb["ln_tmp"][:], in1=lnbanks["rb"],
                            op=ALU.mult), waits=(t_rb,), sig="lnv"))
                P.add("sync", lambda e: e.dma_start(out=dbg[:, 0:S], in_=sb["ln_xb"][:, 0:S]),
                      waits=tuple(tw) + (t_q1,), sig="st", dma=True)
                break
            if seq == 1 and upto == "s1m":
                tbar = Tok("st", P.counts["st"], "none")
                t_v1 = P.add("vector", lambda e: e.tensor_copy(
                    out=sb["ln_xb"][:, 0:TT], in_=sb["xn"][:, 0:TT]), waits=(tbar,), sig="lnv")
                t_q1 = P.add("scalar", lambda e: e.activation(
                    out=sb["ln_xq"][:, 0:TT], in_=sb["xn"][:, 0:TT], func=AF.Square),
                    waits=(tbar,), sig="lna")
                t_m1 = P.add("tensor", lambda e: e.matmul(
                    lnbanks["sum"], sb["ones_col"][:], sb["ln_xb"][:, 0:TT],
                    start=True, stop=True), waits=(t_v1, tbar), sig="lnmm")
                t_m2 = P.add("vector", lambda e: e.tensor_scalar(
                    out=sb["ln_m"][:], in0=lnbanks["sum"], scalar1=1.0 / D, scalar2=None,
                    op0=ALU.mult), waits=(t_m1,), sig="lnv")
                P.add("sync", lambda e: e.dma_start(out=dbg[:, 0:S], in_=sb["ln_xb"][:, 0:S]),
                      waits=(t_m2, t_q1), sig="st", dma=True)
                break
            t_xn = None
            ntt_ln = 1 if (seq == 1 and upto in ("s1a1", "s1abar", "noC", "noTR")) else NQT
            for tt in range(ntt_ln):
                xsrc_seq = 0 if (seq == 1 and upto == "s1a_x") else seq
                P._ln_alt_dest = (seq == 1 and upto == "s1a_x")
                if seq == 1 and upto == "s1a_x":
                    P._lnw["ln"] = Tok("st", P.counts["st"], "none")
                src3 = x_in[xsrc_seq].rearrange("(c p) t -> p c t", p=128)[:, :, TT * tt: TT * (tt + 1)]
                xn_dst = lambda c, tt=tt: sb["xn"][:, S * c + TT * tt: S * c + TT * (tt + 1)]
                t_xn = _emit_ln(P, sb, lnbanks, bank_free, src3, xn_dst, "ln")
            if upto == "p0":
                P.add("sync", lambda e: e.dma_start(out=dbg[:, 0:S], in_=sb["xn"][:, 0:S]),
                      waits=(t_xn,), sig="st", dma=True)
                break
            it = 0
            t_mm = None
            projs_b = projs[:int(upto[1])] if (len(upto) == 2 and upto[0] == "p" and upto[1] in "12345") else projs
            for ip, p in enumerate(projs_b):
                for h in range(NH):
                    sl = it % 2
                    wtb = sb[f"wt{sl}"]
                    t_w = P.add("sync", lambda e, wtb=wtb, p=p, h=h: e.dma_start(
                        out=wtb[:].rearrange("p (c o) -> p c o", c=16), in_=wts[p][h]),
                        waits=(wslot_free.get(("wt", sl)),) + (t_const if it == 0 else ()),
                        sig=f"wt{sl}", dma=True)
                    for tt in range(NQT):
                        pbn = (it * NQT + tt) % 2
                        pb = bank[4 + pbn]
                        esl = (it * NQT + tt) % 2
                        for c in range(16):
                            w = (t_w, t_xn, psum_free.get(("b", pbn))) if c == 0 else ()
                            t_mm = P.add("tensor", lambda e, c=c, pb=pb, wtb=wtb, tt=tt: e.matmul(
                                pb[:], wtb[:, 128 * c:128 * (c + 1)],
                                sb["xn"][:, S * c + TT * tt: S * c + TT * (tt + 1)],
                                start=(c == 0), stop=(c == 15)),
                                waits=w, sig="pmm" if c == 15 else None)
                        peb = sb[f"pe{esl}"]
                        if layer == 0:
                            bcol = sb["bias"][:, 16 * ip + h:16 * ip + h + 1]
                            if p in ("q", "k"):
                                t_ev = _emit_rope(P, sb, bank, pb, peb, esl, tt, t_mm,
                                                  psum_free.get(("pe", esl)), bcol, P._rot_free)
                            else:
                                t_ev = P.add("vector", lambda e, peb=peb, pb=pb, bcol=bcol: e.tensor_scalar(
                                    out=peb[:], in0=pb[:], scalar1=bcol, scalar2=None, op0=ALU.add),
                                    waits=(t_mm, psum_free.get(("pe", esl))), sig="pev")
                        else:
                            t_ev = P.add("vector", lambda e, peb=peb, pb=pb: e.tensor_scalar(
                                out=peb[:], in0=pb[:], scalar1=0.0, scalar2=None, op0=ALU.add),
                                waits=(t_mm, psum_free.get(("pe", esl))), sig="pev")
                        psum_free[("b", pbn)] = t_ev
                        if p == "v":
                            t_last_tr = None
                            for j in range(4):
                                trn = (it * NQT * 4 + tt * 4 + j) % 2
                                trb = bank[6 + trn][:][:, 0:64].bitcast(BF16)
                                vsl = sb[f"vts{trn}"]
                                t_tr = P.add("tensor", lambda e, trb=trb, peb=peb, j=j: e.transpose(
                                    trb, peb[:, 128 * j:128 * (j + 1)], sb["ident"][:]),
                                    waits=(t_ev, P._rot_free.get(("rot", trn)),
                                           P._vts.get(("tr", trn))), sig="vtr")
                                t_cv = P.add("vector", lambda e, vsl=vsl, trb=trb: e.tensor_copy(
                                    out=vsl[:], in_=trb),
                                    waits=(t_tr, P._vts.get(("cv", trn))), sig="pev")
                                P._rot_free[("rot", trn)] = t_cv
                                P._vts[("tr", trn)] = t_cv
                                t_vst = P.add("sync", lambda e, vsl=vsl, h=h, tt=tt, j=j: e.dma_start(
                                    out=scr["v"][h][:, 128 * (4 * tt + j):128 * (4 * tt + j + 1)],
                                    in_=vsl[:]),
                                    waits=(t_cv, scr_free[0]), sig="qkvst", dma=True)
                                P._vts[("cv", trn)] = t_vst
                                t_last_tr = t_tr
                            psum_free[("pe", esl)] = t_last_tr
                        else:
                            t_st = P.add("sync", lambda e, peb=peb, p=p, h=h, tt=tt: e.dma_start(
                                out=scr[p][h][:, TT * tt:TT * (tt + 1)], in_=peb[:]),
                                waits=(t_ev, scr_free[0]), sig="qkvst", dma=True)
                            psum_free[("pe", esl)] = t_st
                    wslot_free[("wt", sl)] = t_mm
                    it += 1
            t_ball = P.tok("qkvst", eng="none")
            P._xn_free = t_mm
            if seq == 1 and upto == "s1b":
                P.add("sync", lambda e: e.dma_start(out=dbg[:, 0:S], in_=scr[projs[1]][3]),
                      waits=(t_ball,), sig="st", dma=True)
                break
            if upto == "proj" or (len(upto) == 2 and upto[0] == "p" and upto[1] in "12345"):
                P.add("sync", lambda e: e.dma_start(out=dbg[:, 0:S], in_=scr[projs_b[-1]][3]),
                      waits=(t_ball,), sig="st", dma=True)
                break
            ebuse = {}
            sc_free = {}
            for h in range(0 if upto == "noC" else NH):
                hl = h % 2
                t_ld = []
                for p in projs:
                    if p == "v":
                        continue
                    t_ld.append(P.add("sync", lambda e, p=p, h=h, hl=hl: e.dma_start(
                        out=sb[f"hd_{p}{hl}"][:], in_=scr[p][h]),
                        waits=(t_ball, head_free.get(hl)), sig=f"hl{hl}", dma=True))
                t_ld.append(P.add("sync", lambda e, h=h, hl=hl: e.dma_start(
                    out=sb[f"hd_v{hl}"][:], in_=scr["v"][h]),
                    waits=(t_ball, head_free.get(hl)), sig=f"hl{hl}", dma=True))
                t_ldall = P.tok(f"hl{hl}", eng="none")
                qn = ["q"] if layer == 0 else ["q1", "q2"]
                kn = ["k"] if layer == 0 else ["k1", "k2"]
                t_avlast = None
                for qt in range(NQT):
                    kcmax = 4 * qt + 4 if layer == 0 else 16
                    def av_pair(kc, st_, h=h, hl=hl, qt=qt, kcmax=kcmax):
                        cbk = bank[6 + st_]
                        smb = bank[st_][0:1, :]
                        ebuf = sb[f"eb{st_}{kc % 2}"]
                        t_e = ebuse[("e", st_, kc % 2)]
                        w_extra = []
                        if kc == 0:
                            w_extra.append(psum_free.get(("ctx", st_)))
                            w_extra.append(bank_free.get("ln_sum" if st_ == 0 else "ln_sq"))
                            w_extra.append(P._rot_free.get(("rot", st_)))
                        t_av = P.add("tensor", lambda e, cbk=cbk, ebuf=ebuf, hl=hl, kc=kc: e.matmul(
                            cbk[:], sb[f"hd_v{hl}"][:, 128 * kc:128 * (kc + 1)], ebuf[:],
                            start=(kc == 0), stop=(kc == kcmax - 1)),
                            waits=(t_e,) + tuple(w_extra), sig="avm")
                        t_sm = P.add("tensor", lambda e, smb=smb, ebuf=ebuf: e.matmul(
                            smb, sb["ones_col"][:], ebuf[:],
                            start=(kc == 0), stop=(kc == kcmax - 1)), sig="avm")
                        ebuse[("c", st_, kc % 2)] = t_sm
                        return t_sm
                    for kc in range(kcmax):
                        for st_ in range(nset):
                            scn = (kc * nset + st_) % 2
                            scb = bank[4 + scn]
                            w = [t_ldall, sc_free.get(scn)]
                            if kc == 0 and st_ == 0:
                                w.append(psum_free.get(("b", 0)))
                                w.append(psum_free.get(("b", 1)))
                            t_s = P.add("tensor", lambda e, scb=scb, hl=hl, kc=kc, qt=qt, kname=kn[st_], qname=qn[st_]: e.matmul(
                                scb[:], sb[f"hd_{kname}{hl}"][:, 128 * kc:128 * (kc + 1)],
                                sb[f"hd_{qname}{hl}"][:, TT * qt:TT * (qt + 1)],
                                start=True, stop=True),
                                waits=tuple(w), sig="smm")
                            t_pre = t_s
                            if layer == 0 and kc >= 4 * qt:
                                j = kc - 4 * qt
                                t_pre = P.add("vector", lambda e, scb=scb, j=j: e.tensor_tensor(
                                    out=scb[:], in0=scb[:], in1=sb["mask"][:, TT * j:TT * (j + 1)],
                                    op=ALU.add), waits=(t_s,), sig="mskv")
                            ebuf = sb[f"eb{st_}{kc % 2}"]
                            t_e = P.add("scalar", lambda e, ebuf=ebuf, scb=scb: e.activation(
                                out=ebuf[:], in_=scb[:], func=AF.Exp, scale=1.0 / (128.0 ** 0.5)),
                                waits=(t_pre, ebuse.get(("c", st_, kc % 2))), sig="exa")
                            ebuse[("e", st_, kc % 2)] = t_e
                            sc_free[scn] = t_e
                        if kc >= 1:
                            for st_ in range(nset):
                                t_avlast = av_pair(kc - 1, st_)
                    for st_ in range(nset):
                        t_avlast = av_pair(kcmax - 1, st_)
                    par = qt % 2
                    cbuf0, tbuf0 = sb[f"cb{par}"], sb[f"tb{par}"]
                    t_cp1 = P.add("vector", lambda e: e.tensor_copy(out=sb["r1row"][:], in_=bank[0][0:1, :]),
                                  waits=(t_avlast,), sig="cvv")
                    P.add("scalar", lambda e: e.activation(out=sb["r1row"][:], in_=sb["r1row"][:],
                                                           func=AF.Ln), waits=(t_cp1,))
                    t_i1 = P.add("scalar", lambda e: e.activation(out=sb["r1row"][:], in_=sb["r1row"][:],
                                                                  func=AF.Exp, scale=-1.0), sig="cva")
                    t_b1 = P.add("tensor", lambda e: e.matmul(bank[2][:], sb["ones1_f32"][:], sb["r1row"][:],
                                                              start=True, stop=True),
                                 waits=(t_i1, bank_free.get("ln_mb")), sig="bcm")
                    t_c1 = P.add("vector", lambda e, cbuf0=cbuf0: e.tensor_copy(out=cbuf0[:], in_=bank[6][:]),
                                 waits=(t_avlast, bank_free.get(f"cb{par}")), sig="cvv")
                    psum_free[("ctx", 0)] = t_c1
                    bank_free["ln_sum"] = t_cp1
                    if layer == 0:
                        t_ctx = P.add("vector", lambda e, cbuf0=cbuf0, h=h, qt=qt: e.tensor_tensor(
                            out=ctx_all[:, S * h + TT * qt: S * h + TT * (qt + 1)],
                            in0=cbuf0[:], in1=bank[2][:], op=ALU.mult),
                            waits=(t_b1, P._ctx_war), sig="ctxv")
                        bank_free["ln_mb"] = t_ctx
                        bank_free[f"cb{par}"] = t_ctx
                    else:
                        t_cp2 = P.add("vector", lambda e: e.tensor_copy(out=sb["r2row"][:], in_=bank[1][0:1, :]),
                                      waits=(t_avlast,), sig="cvv")
                        P.add("scalar", lambda e: e.activation(out=sb["r2row"][:], in_=sb["r2row"][:],
                                                               func=AF.Ln), waits=(t_cp2,))
                        t_i2 = P.add("scalar", lambda e: e.activation(out=sb["r2row"][:], in_=sb["r2row"][:],
                                                                      func=AF.Exp, scale=-1.0), sig="cva")
                        t_i2b = P.add("vector", lambda e, h=h: e.tensor_scalar(
                            out=sb["r2row"][:], in0=sb["r2row"][:], scalar1=sb["lam"][0:1, h:h + 1],
                            scalar2=-1.0, op0=ALU.mult, op1=ALU.mult), waits=(t_i2,), sig="cvv")
                        bank_free["ln_sq"] = t_cp2
                        t_b2 = P.add("tensor", lambda e: e.matmul(bank[3][:], sb["ones1_f32"][:], sb["r2row"][:],
                                                                  start=True, stop=True),
                                     waits=(t_i2b, bank_free.get("ln_rb")), sig="bcm")
                        t_t1 = P.add("vector", lambda e, cbuf0=cbuf0, tbuf0=tbuf0: e.tensor_tensor(
                            out=tbuf0[:], in0=cbuf0[:], in1=bank[2][:], op=ALU.mult),
                            waits=(t_b1,), sig="cvv")
                        bank_free["ln_mb"] = t_t1
                        par1 = (qt + 1) % 2
                        cbuf1 = sb[f"cb{par1}"]
                        t_c2 = P.add("vector", lambda e, cbuf1=cbuf1: e.tensor_copy(out=cbuf1[:], in_=bank[7][:]),
                                     waits=(t_avlast, bank_free.get(f"cb{par1}")), sig="cvv")
                        psum_free[("ctx", 1)] = t_c2
                        t_t2 = P.add("vector", lambda e, cbuf1=cbuf1, par1=par1: e.tensor_tensor(
                            out=sb[f"tb{par1}"][:], in0=cbuf1[:], in1=bank[3][:], op=ALU.mult),
                            waits=(t_b2, bank_free.get(f"tb{par1}")), sig="cvv")
                        bank_free["ln_rb"] = t_t2
                        t_ctx = P.add("vector", lambda e, tbuf0=tbuf0, par1=par1, h=h, qt=qt: e.tensor_tensor(
                            out=ctx_all[:, S * h + TT * qt: S * h + TT * (qt + 1)],
                            in0=tbuf0[:], in1=sb[f"tb{par1}"][:], op=ALU.add),
                            waits=(P._ctx_war,), sig="ctxv")
                        bank_free[f"cb{par}"] = t_ctx
                        bank_free[f"cb{par1}"] = t_ctx
                        bank_free[f"tb{par1}"] = t_ctx
                    bank_free[f"tb{par}"] = t_ctx
                    P._rot_free[("rot", 0)] = t_c1
                    if layer != 0:
                        P._rot_free[("rot", 1)] = t_c2
                head_free[hl] = t_avlast
            t_call = P.tok("ctxv", eng="vector")
            scr_free[0] = (P.tok("hl0", eng="none"), P.tok("hl1", eng="none"))
            if upto == "attnc" or (seq == 1 and upto == "s1c"):
                P.add("gpsimd", lambda e: e.dma_start(out=dbg, in_=ctx_all[:]),
                      waits=(t_call,), sig="st", dma=True)
                break
            for o2 in range(16):
                sl = o2 % 2
                wtb = sb[f"wt{sl}"]
                t_w = P.add("sync", lambda e, wtb=wtb, o2=o2: e.dma_start(
                    out=wtb[:].rearrange("p (c o) -> p c o", c=16), in_=wo[o2]),
                    waits=(wslot_free.get(("wt", sl)),), sig=f"wt{sl}", dma=True)
                for tt in range(NQT):
                    pbn = (o2 * NQT + tt) % 2
                    pb = bank[4 + pbn]
                    for c in range(16):
                        w = (t_w, t_call, psum_free.get(("b", pbn)), sc_free.get(0), sc_free.get(1)) if c == 0 else ()
                        t_mm = P.add("tensor", lambda e, c=c, pb=pb, wtb=wtb, tt=tt: e.matmul(
                            pb[:], wtb[:, 128 * c:128 * (c + 1)],
                            ctx_all[:, S * c + TT * tt: S * c + TT * (tt + 1)],
                            start=(c == 0), stop=(c == 15)),
                            waits=w, sig="omm" if c == 15 else None)
                    xrb = sb[f"xr{(o2 * NQT + tt) % 2}"]
                    src = x_in[seq].rearrange("(c p) t -> p c t", p=128)[:, o2, TT * tt: TT * (tt + 1)]
                    t_xr = P.add("sync", lambda e, xrb=xrb, src=src: e.dma_start(out=xrb[:], in_=src),
                                 waits=(bank_free.get(f"xr{(o2 * NQT + tt) % 2}"),),
                                 sig=f"xr{(o2 * NQT + tt) % 2}", dma=True)
                    fob = sb[f"fo{(o2 * NQT + tt) % 2}"]
                    if layer == 0:
                        bocol = sb["bias"][:, 16 * len(projs) + o2:16 * len(projs) + o2 + 1]
                        P.add("vector", lambda e, fob=fob, pb=pb, bocol=bocol: e.tensor_scalar(
                            out=fob[:], in0=pb[:], scalar1=bocol, scalar2=None, op0=ALU.add),
                            waits=(t_mm, bank_free.get(f"fo{(o2 * NQT + tt) % 2}")))
                        t_fo = P.add("vector", lambda e, fob=fob, xrb=xrb: e.tensor_tensor(
                            out=fob[:], in0=fob[:], in1=xrb[:], op=ALU.add),
                            waits=(t_xr,), sig="fov")
                    else:
                        t_fo = P.add("vector", lambda e, fob=fob, xrb=xrb, pb=pb: e.tensor_tensor(
                            out=fob[:], in0=xrb[:], in1=pb[:], op=ALU.add),
                            waits=(t_mm, t_xr, bank_free.get(f"fo{(o2 * NQT + tt) % 2}")), sig="fov")
                    psum_free[("b", pbn)] = t_fo
                    bank_free[f"xr{(o2 * NQT + tt) % 2}"] = t_fo
                    dst = x_out[seq].rearrange("(c p) t -> p c t", p=128)[:, o2, TT * tt: TT * (tt + 1)]
                    t_st = P.add("sync", lambda e, fob=fob, dst=dst: e.dma_start(out=dst, in_=fob[:]),
                                 waits=(t_fo,), sig="st", dma=True)
                    bank_free[f"fo{(o2 * NQT + tt) % 2}"] = t_st
                wslot_free[("wt", sl)] = t_mm
            P._ctx_war = Tok("omm", P.counts["omm"], "tensor")
            P._lnw["ln"] = Tok("omm", P.counts["omm"], "tensor")
        P.add("sync", lambda e: e.nop(), waits=(Tok("st", P.counts["st"], "none"),))
        P.emit()
    return nc


def attn_inputs(layer, x_fm, w, c):
    bf = lambda a: np.asarray(a).astype(BF)
    if layer == 0:
        return dict(x_in=x_fm, w_q=bf(w["wqT"]), w_k=bf(w["wkT"]), w_v=bf(w["wvT"]),
                    w_o=bf(w["woT"]),
                    b_q=w["bq"], b_k=w["bk"], b_v=w["bv"], b_o=w["bo"],
                    cos_fm=c["cos_fm"], sin_fm=c["sin_fm"], masks=c["masks"], ropeR=c["ropeR_bf"],
                    ident=c["ident_bf"],
                    ones_col=c["ones_col_bf"], ones_row=c["ones_row_bf"], ones1=c["ones1_f32"])
    return dict(x_in=x_fm, w_q1=bf(w["q1T"]), w_k1=bf(w["k1T"]), w_q2=bf(w["q2T"]),
                w_k2=bf(w["k2T"]), w_v=bf(w["vT"]), w_o=bf(w["oT"]), lam=w["lam"],
                ident=c["ident_bf"],
                ones_col=c["ones_col_bf"], ones_row=c["ones_row_bf"], ones1=c["ones1_f32"])


def build_moe(nseq=NSEQ, upto="full", nblk=S // TBLK):
    nc = bass.Bass("TRN2", target_bir_lowering=False, detect_race_conditions=False)
    x_in = nc.dram_tensor("x_in", [nseq, D, S], F32, kind="ExternalInput").ap()
    gTb = nc.dram_tensor("gT_bf", [NE, IM // 128, 128, D // 128, 128], BF16, kind="ExternalInput").ap()
    uTb = nc.dram_tensor("uT_bf", [NE, IM // 128, 128, D // 128, 128], BF16, kind="ExternalInput").ap()
    dTb = nc.dram_tensor("dT_bf", [NE, D // 128, 128, IM // 128, 128], BF16, kind="ExternalInput").ap()
    cls_in = nc.dram_tensor("clsT_p", [128, 16 * 128], F32, kind="ExternalInput").ap()
    ebias_in = nc.dram_tensor("ebias", [1, 12], F32, kind="ExternalInput").ap()
    esel_in = nc.dram_tensor("esel", [9, 9 * 128], BF16, kind="ExternalInput").ap()
    ident_in = nc.dram_tensor("ident", [128, 128], BF16, kind="ExternalInput").ap()
    ones_col_in = nc.dram_tensor("ones_col", [128, 1], BF16, kind="ExternalInput").ap()
    ones1_in = nc.dram_tensor("ones1", [1, 128], F32, kind="ExternalInput").ap()
    x_out = nc.dram_tensor("x_out", [nseq, D, S], F32, kind="ExternalOutput").ap()
    dbg = None
    if upto != "full":
        dbg = nc.dram_tensor("dbg", [128, TBLK], F32, kind="ExternalOutput").ap()

    NIC = IM // 128
    with ExitStack() as stk:
        e = stk.enter_context
        sb = {}
        sb["ones_col"] = e(nc.sbuf_tensor("ones_col_sb", [128, 1], BF16))
        sb["ones1_f32"] = e(nc.sbuf_tensor("ones1_sb", [1, 128], F32))
        sb["ident"] = e(nc.sbuf_tensor("ident_sb", [128, 128], BF16))
        sb["cls"] = e(nc.sbuf_tensor("cls_sb", [128, 16 * 128], BF16))
        sb["esel"] = e(nc.sbuf_tensor("esel_sb", [9, 9 * 128], BF16))
        sb["ebias_b"] = e(nc.sbuf_tensor("ebias_b", [128, 12], F32))
        sb["ln_tmp"] = e(nc.sbuf_tensor("ln_tmp", [128, TT], F32))
        for nm in ("ln_m", "ln_t2", "ln_v", "ln_s", "ln_r", "ln_sqr"):
            sb[nm] = e(nc.sbuf_tensor(nm, [1, TT], F32))
        sb["xn"] = e(nc.sbuf_tensor("xn", [128, 16 * TBLK], BF16))
        sb["yac"] = e(nc.sbuf_tensor("yac", [128, 16 * TBLK], F32))
        sb["sact"] = e(nc.sbuf_tensor("sact", [128, NIC * TBLK], BF16))
        sb["ln_xf"] = sb["yac"][:][:, 0:16 * TT]
        sb["ln_xb"] = sb["yac"][:][:, 16 * TT:16 * TT + 8 * TT].bitcast(BF16)
        sb["ln_xq"] = sb["yac"][:][:, 16 * TT + 8 * TT:16 * TT + 16 * TT].bitcast(BF16)
        for sl in range(2):
            sb[f"wg{sl}"] = e(nc.sbuf_tensor(f"wg{sl}", [128, 16 * 128], BF16))
            sb[f"wu{sl}"] = e(nc.sbuf_tensor(f"wu{sl}", [128, 16 * 128], BF16))
            sb[f"wd{sl}"] = e(nc.sbuf_tensor(f"wd{sl}", [128, NIC * 128], BF16))
            sb[f"sg{sl}"] = e(nc.sbuf_tensor(f"sg{sl}", [128, TT], BF16))
            sb[f"h1{sl}"] = e(nc.sbuf_tensor(f"h1{sl}", [128, TT], BF16))
            sb[f"wb{sl}"] = e(nc.sbuf_tensor(f"wb{sl}", [128, TT], BF16))
            sb[f"xr{sl}"] = e(nc.sbuf_tensor(f"xr{sl}", [128, TT], F32))
            sb[f"fo{sl}"] = e(nc.sbuf_tensor(f"fo{sl}", [128, TT], F32))
            sb[f"ft{sl}"] = e(nc.sbuf_tensor(f"ft{sl}", [128, TT], F32))
        for nm, shape, dt in (("g_el", [128, 12], F32), ("g_pr", [128, 12], F32),
                              ("g_bb", [128, 12], F32), ("g_b2", [128, 12], F32),
                              ("g_k1", [128, 12], F32), ("g_k2", [128, 12], F32),
                              ("g_w", [128, 12], F32), ("g_w9", [128, 128], BF16),
                              ("g_nm", [128, 1], F32), ("g_sume", [128, 1], F32),
                              ("g_z", [128, 1], F32), ("g_lgA", [128, 12], F32), ("g_lgB", [128, 12], F32),
                              ("g_t6", [128, 6], F32), ("g_t3", [128, 3], F32), ("g_t2", [128, 2], F32),
                              ("g_t12", [128, 12], F32), ("g_big", [128, 12], F32),
                              ("g_one", [128, 12], F32), ("g_t6b", [128, 6], F32),
                              ("g_t3b", [128, 3], F32), ("g_smb", [128, 1], F32),
                              ("g_prb", [128, 12], F32), ("g_bbb", [128, 12], F32),
                              ("g_b2b", [128, 12], F32), ("g_m1b", [128, 1], F32),
                              ("g_m2b", [128, 1], F32), ("g_k2b", [128, 12], F32),
                              ("g_sel", [128, 12], F32), ("g_selb", [128, 12], F32),
                              ("g_wb", [128, 12], F32), ("g_t2b", [128, 2], F32),
                              ("g_rps", [128, 1], F32), ("g_m1", [128, 1], F32),
                              ("g_m2", [128, 1], F32)):
            sb[nm] = e(nc.sbuf_tensor(nm, shape, dt))
        sb["w9T"] = e(nc.sbuf_tensor("w9T", [9, TBLK], BF16))
        bank = [e(nc.psum_tensor(f"bank{i}", [128, TT], F32)) for i in range(8)]

        P = Prog(nc, stk)
        P._lnw = {}
        P._xn_free = None
        P._w9war = None
        P._sact_war = None
        P._yac_war = None
        bank_free = {}
        psum_free = {}
        wslot_free = {}
        lnbanks = {"sum": bank[0][0:1, :], "sq": bank[1][0:1, :], "mb": bank[2][:], "rb": bank[3][:]}

        tc = [P.add("sync", lambda e: e.dma_start(out=sb["ones_col"][:], in_=ones_col_in), sig="cst", dma=True),
              P.add("sync", lambda e: e.dma_start(out=sb["ones1_f32"][:], in_=ones1_in), sig="cst", dma=True),
              P.add("sync", lambda e: e.dma_start(out=sb["ident"][:], in_=ident_in), sig="cst", dma=True),
              P.add("sync", lambda e: e.dma_start(out=sb["esel"][:], in_=esel_in), sig="cst", dma=True),
              P.add("gpsimd", lambda e: e.dma_start(out=sb["cls"][:], in_=cls_in), sig="cstg", dma=True)]
        eb_row = e(nc.sbuf_tensor("eb_row", [1, 12], F32))
        tc.append(P.add("sync", lambda e: e.dma_start(out=eb_row[:], in_=ebias_in), sig="cst", dma=True))
        t_ebm = P.add("tensor", lambda e: e.matmul(bank[4][:, 0:12], sb["ones1_f32"][:], eb_row[:],
                                                   start=True, stop=True), waits=tuple(tc), sig="mmg")
        t_ebc = P.add("vector", lambda e: e.tensor_copy(out=sb["ebias_b"][:], in_=bank[4][:, 0:12]),
                      waits=(t_ebm,), sig="gv")
        P.add("vector", lambda e: e.memset(sb["g_big"][:], 1e30))
        P.add("vector", lambda e: e.memset(sb["g_one"][:], 1.0))
        psum_free[("gu", 0)] = t_ebc
        t_const = tuple(tc) + (t_ebc,)

        for seq in range(nseq):
            for bi in range(nblk):
                t0 = bi * TBLK
                t_xn = None
                for tt in range(TBLK // TT):
                    src3 = x_in[seq].rearrange("(c p) t -> p c t", p=128)[:, :, t0 + TT * tt: t0 + TT * (tt + 1)]
                    xn_dst = lambda c, tt=tt: sb["xn"][:, TBLK * c + TT * tt: TBLK * c + TT * (tt + 1)]
                    t_xn = _emit_ln(P, sb, lnbanks, bank_free, src3, xn_dst, "ln")
                if upto == "lnonly":
                    P.add("gpsimd", lambda e: e.dma_start(out=dbg[:, 0:1024], in_=sb["xn"][:, 0:1024]),
                          waits=(t_xn,), sig="st", dma=True)
                    break
                t_w9 = None
                for tch in range(1 if upto == "gatew" else TBLK // 128):
                    lb = bank[4][:, 0:12]
                    lbf = bank[4][:, 0:128]
                    t_lg = None
                    for c in range(16):
                        w = (t_xn, t_const, psum_free.get(("gu", 0))) if c == 0 else ()
                        t_lg = P.add("tensor", lambda e, c=c, tch=tch, lbf=lbf: e.matmul(
                            lbf, sb["xn"][:, TBLK * c + 128 * tch: TBLK * c + 128 * (tch + 1)],
                            sb["cls"][:, 128 * c:128 * (c + 1)],
                            start=(c == 0), stop=(c == 15)),
                            waits=w, sig="mmg" if c == 15 else None)
                    def gb(dst, src, t):
                        w = dst.shape[-1]
                        return P.add("gpsimd", lambda e, d=dst, s=src, w=w: e.tensor_tensor(
                            out=d, in0=s, in1=sb["g_one"][:, 0:w], op=ALU.mult),
                            waits=(t,), sig="gp")

                    t_el = P.add("scalar", lambda e, lb=lb: e.activation(
                        out=sb["g_el"][:], in_=lb, func=AF.Exp), waits=(t_lg,), sig="ga")
                    psum_free[("gu", 0)] = t_el
                    t = P.add("vector", lambda e: e.tensor_tensor(
                        out=sb["g_t6"][:], in0=sb["g_el"][:, 0:6], in1=sb["g_el"][:, 6:12],
                        op=ALU.add), waits=(t_el,), sig="gv")
                    t = gb(sb["g_t6b"][:], sb["g_t6"][:], t)
                    t = P.add("vector", lambda e: e.tensor_tensor(
                        out=sb["g_t3"][:], in0=sb["g_t6b"][:, 0:3], in1=sb["g_t6b"][:, 3:6],
                        op=ALU.add), waits=(t,), sig="gv")
                    t = gb(sb["g_t3b"][:], sb["g_t3"][:], t)
                    t = P.add("vector", lambda e: e.tensor_tensor(
                        out=sb["g_sume"][:], in0=sb["g_t3b"][:, 0:1], in1=sb["g_t3b"][:, 1:2],
                        op=ALU.add), waits=(t,), sig="gv")
                    t = gb(sb["g_smb"][:], sb["g_sume"][:], t)
                    t = P.add("vector", lambda e: e.tensor_tensor(
                        out=sb["g_sume"][:], in0=sb["g_smb"][:], in1=sb["g_t3b"][:, 2:3],
                        op=ALU.add), waits=(t,), sig="gv")
                    t = P.add("scalar", lambda e: e.activation(
                        out=sb["g_rps"][:], in_=sb["g_sume"][:], func=AF.Ln), waits=(t,), sig="ga")
                    t = gb(sb["g_smb"][:], sb["g_rps"][:], t)
                    t_rp = P.add("scalar", lambda e: e.activation(
                        out=sb["g_rps"][:], in_=sb["g_smb"][:], func=AF.Exp, scale=-1.0),
                        waits=(t,), sig="ga")
                    t_pr = P.add("vector", lambda e: e.tensor_scalar(
                        out=sb["g_pr"][:], in0=sb["g_el"][:], scalar1=sb["g_rps"][:], scalar2=None,
                        op0=ALU.mult), waits=(t_rp,), sig="gv")
                    t = gb(sb["g_prb"][:], sb["g_pr"][:], t_pr)
                    t_bb = P.add("vector", lambda e: e.tensor_tensor(
                        out=sb["g_bb"][:], in0=sb["g_prb"][:], in1=sb["ebias_b"][:], op=ALU.add),
                        waits=(t,), sig="gv")

                    def rowmax12(srcb, dst, dstb, t_prev):
                        t = P.add("vector", lambda e: e.tensor_tensor(
                            out=sb["g_t6"][:], in0=srcb[:, 0:6], in1=srcb[:, 6:12], op=ALU.max),
                            waits=(t_prev,), sig="gv")
                        t = gb(sb["g_t6b"][:], sb["g_t6"][:], t)
                        t = P.add("vector", lambda e: e.tensor_tensor(
                            out=sb["g_t3"][:], in0=sb["g_t6b"][:, 0:3], in1=sb["g_t6b"][:, 3:6],
                            op=ALU.max), waits=(t,), sig="gv")
                        t = gb(sb["g_t3b"][:], sb["g_t3"][:], t)
                        t = P.add("vector", lambda e: e.tensor_tensor(
                            out=dst, in0=sb["g_t3b"][:, 0:1], in1=sb["g_t3b"][:, 1:2], op=ALU.max),
                            waits=(t,), sig="gv")
                        t = gb(sb["g_smb"][:], dst, t)
                        t = P.add("vector", lambda e: e.tensor_tensor(
                            out=dst, in0=sb["g_smb"][:], in1=sb["g_t3b"][:, 2:3], op=ALU.max),
                            waits=(t,), sig="gv")
                        return gb(dstb, dst, t)

                    t = gb(sb["g_bbb"][:], sb["g_bb"][:], t_bb)
                    t = rowmax12(sb["g_bbb"], sb["g_m1"][:], sb["g_m1b"][:], t)
                    t = P.add("vector", lambda e: e.tensor_scalar(
                        out=sb["g_t12"][:], in0=sb["g_bbb"][:], scalar1=sb["g_m1b"][:],
                        scalar2=None, op0=ALU.subtract), waits=(t,), sig="gv")
                    t = P.add("scalar", lambda e: e.activation(
                        out=sb["g_t12"][:], in_=sb["g_t12"][:], func=AF.Sign), waits=(t,), sig="ga")
                    t_k1 = P.add("vector", lambda e: e.tensor_scalar(
                        out=sb["g_k1"][:], in0=sb["g_t12"][:], scalar1=1.0, scalar2=1.0,
                        op0=ALU.add, op1=ALU.min), waits=(t,), sig="gv")
                    t = P.add("gpsimd", lambda e: e.tensor_tensor(
                        out=sb["g_b2"][:], in0=sb["g_k1"][:], in1=sb["g_big"][:], op=ALU.mult),
                        waits=(t_k1,), sig="gp")
                    t = P.add("vector", lambda e: e.tensor_tensor(
                        out=sb["g_b2"][:], in0=sb["g_bbb"][:], in1=sb["g_b2"][:], op=ALU.subtract),
                        waits=(t,), sig="gv")
                    t = gb(sb["g_b2b"][:], sb["g_b2"][:], t)
                    t = rowmax12(sb["g_b2b"], sb["g_m2"][:], sb["g_m2b"][:], t)
                    t = P.add("vector", lambda e: e.tensor_scalar(
                        out=sb["g_t12"][:], in0=sb["g_b2b"][:], scalar1=sb["g_m2b"][:],
                        scalar2=None, op0=ALU.subtract), waits=(t,), sig="gv")
                    t = P.add("scalar", lambda e: e.activation(
                        out=sb["g_t12"][:], in_=sb["g_t12"][:], func=AF.Sign), waits=(t,), sig="ga")
                    t = P.add("vector", lambda e: e.tensor_scalar(
                        out=sb["g_k2"][:], in0=sb["g_t12"][:], scalar1=1.0, scalar2=1.0,
                        op0=ALU.add, op1=ALU.min), waits=(t,), sig="gv")
                    t = gb(sb["g_k2b"][:], sb["g_k2"][:], t)
                    t = P.add("vector", lambda e: e.tensor_tensor(
                        out=sb["g_sel"][:], in0=sb["g_k1"][:], in1=sb["g_k2b"][:], op=ALU.add),
                        waits=(t,), sig="gv")
                    t = gb(sb["g_selb"][:], sb["g_sel"][:], t)
                    t_gww = P.add("vector", lambda e: e.scalar_tensor_tensor(
                        out=sb["g_w"][:], in0=sb["g_pr"][:], scalar=0.75, in1=sb["g_selb"][:],
                        op0=ALU.mult, op1=ALU.mult), waits=(t,), sig="gv")
                    if upto == "gatew" and tch == 0:
                        P.add("sync", lambda e: e.dma_start(out=dbg[:, 0:12], in_=sb["g_w"][:]),
                              waits=(t_gww,), sig="st", dma=True)
                        P.add("sync", lambda e: e.dma_start(out=dbg[:, 16:28], in_=sb["g_pr"][:]),
                              waits=(t_gww,), sig="st", dma=True)
                        P.add("sync", lambda e: e.dma_start(out=dbg[:, 32:44], in_=sb["g_bb"][:]),
                              waits=(t_gww,), sig="st", dma=True)
                        P.add("sync", lambda e: e.dma_start(out=dbg[:, 48:60], in_=sb["g_sel"][:]),
                              waits=(t_gww,), sig="st", dma=True)
                    t = gb(sb["g_wb"][:], sb["g_w"][:], t_gww)
                    t_w8 = P.add("vector", lambda e: e.tensor_copy(
                        out=sb["g_w9"][:, 0:8], in_=sb["g_wb"][:, 0:8]),
                        waits=(t, P._w9war), sig="gv")
                    t = P.add("vector", lambda e: e.tensor_tensor(
                        out=sb["g_t2"][:], in0=sb["g_wb"][:, 8:10], in1=sb["g_wb"][:, 10:12],
                        op=ALU.add), waits=(t,), sig="gv")
                    t = gb(sb["g_t2b"][:], sb["g_t2"][:], t)
                    t = P.add("vector", lambda e: e.tensor_tensor(
                        out=sb["g_z"][:], in0=sb["g_t2b"][:, 0:1], in1=sb["g_t2b"][:, 1:2],
                        op=ALU.add), waits=(t,), sig="gv")
                    t = gb(sb["g_smb"][:], sb["g_z"][:], t)
                    t_gw = P.add("vector", lambda e: e.tensor_copy(
                        out=sb["g_w9"][:, 8:9], in_=sb["g_smb"][:]), waits=(t, t_w8), sig="gv")
                    trb = bank[5][:].bitcast(BF16)[:, 0:128]
                    t_tr = P.add("tensor", lambda e, trb=trb: e.transpose(
                        trb, sb["g_w9"][:], sb["ident"][:]),
                        waits=(t_gw, psum_free.get(("w9t", 0))), sig="mmg")
                    t_w9 = P.add("vector", lambda e, tch=tch, trb=trb: e.tensor_copy(
                        out=sb["w9T"][0:9, 128 * tch:128 * (tch + 1)], in_=trb[0:9, :]),
                        waits=(t_tr,), sig="gv")
                    psum_free[("w9t", 0)] = t_w9
                    if upto == "gatew" and tch == 0:
                        P.add("sync", lambda e: e.dma_start(out=dbg[:, 384:396], in_=sb["g_bb"][:]),
                              waits=(t_w9,), sig="st", dma=True)
                        P.add("sync", lambda e: e.dma_start(out=dbg[:, 400:412], in_=sb["g_k1"][:]),
                              waits=(t_w9,), sig="st", dma=True)
                        P.add("sync", lambda e: e.dma_start(out=dbg[:, 416:428], in_=sb["ebias_b"][:]),
                              waits=(t_w9,), sig="st", dma=True)
                P._w9war = None
                psum_free[("u", 1)] = t_w9
                t_w9all = P.tok("gv", eng="vector")
                if upto == "gate":
                    P.add("gpsimd", lambda e: e.dma_start(out=dbg[0:9, :], in_=sb["w9T"][0:9, :]),
                          waits=(t_w9,), sig="st", dma=True)
                    break
                t_sact_last = None
                t_dn_last = {}
                for ei in range(NE):
                    t_wb = {}
                    for tt in range(TBLK // TT):
                        wbk = bank[6 + tt][:]
                        t_bm = P.add("tensor", lambda e, ei=ei, tt=tt, wbk=wbk: e.matmul(
                            wbk, sb["esel"][:, 128 * ei:128 * (ei + 1)],
                            sb["w9T"][0:9, TT * tt:TT * (tt + 1)],
                            start=True, stop=True),
                            waits=(t_w9, psum_free.get(("wb", tt))), sig="mmg")
                        t_wb[tt] = P.add("vector", lambda e, tt=tt, wbk=wbk: e.tensor_copy(
                            out=sb[f"wb{tt}"][:], in_=wbk),
                            waits=(t_bm, t_dn_last.get(("wb", tt))), sig="gv")
                        psum_free[("wb", tt)] = t_wb[tt]
                    for ic in range(NIC):
                        sl = ic % 2
                        wgb, wub = sb[f"wg{sl}"], sb[f"wu{sl}"]
                        t_wg = P.add("sync", lambda e, wgb=wgb, ei=ei, ic=ic: e.dma_start(
                            out=wgb[:].rearrange("p (c o) -> p c o", c=16), in_=gTb[ei, ic]),
                            waits=(wslot_free.get(("g", sl)),), sig=f"wg{sl}", dma=True)
                        t_wu = P.add("sync", lambda e, wub=wub, ei=ei, ic=ic: e.dma_start(
                            out=wub[:].rearrange("p (c o) -> p c o", c=16), in_=uTb[ei, ic]),
                            waits=(wslot_free.get(("u", sl)),), sig=f"wu{sl}", dma=True)
                        for tt in range(TBLK // TT):
                            gbn = tt
                            gb = bank[2 + tt][:]
                            ub = bank[4 + tt][:]
                            t_gmm = t_umm = None
                            for c in range(16):
                                w = (t_wg, t_xn, psum_free.get(("g", gbn)),
                                     bank_free.get("ln_mb") if tt == 0 else bank_free.get("ln_rb")) if c == 0 else ()
                                t_gmm = P.add("tensor", lambda e, c=c, gb=gb, wgb=wgb, tt=tt: e.matmul(
                                    gb, wgb[:, 128 * c:128 * (c + 1)],
                                    sb["xn"][:, TBLK * c + TT * tt: TBLK * c + TT * (tt + 1)],
                                    start=(c == 0), stop=(c == 15)),
                                    waits=w, sig="gmm" if c == 15 else None)
                            for c in range(16):
                                w = (t_wu, t_xn, psum_free.get(("u", gbn)), psum_free.get(("gu", 0))) if c == 0 else ()
                                t_umm = P.add("tensor", lambda e, c=c, ub=ub, wub=wub, tt=tt: e.matmul(
                                    ub, wub[:, 128 * c:128 * (c + 1)],
                                    sb["xn"][:, TBLK * c + TT * tt: TBLK * c + TT * (tt + 1)],
                                    start=(c == 0), stop=(c == 15)),
                                    waits=w, sig="umm" if c == 15 else None)
                            if tt == 1:
                                wslot_free[("g", sl)] = t_gmm
                                wslot_free[("u", sl)] = t_umm
                            esl2 = tt
                            sgb, h1b = sb[f"sg{esl2}"], sb[f"h1{esl2}"]
                            t_sg = P.add("scalar", lambda e, sgb=sgb, gb=gb: e.activation(
                                out=sgb[:], in_=gb, func=AF.Silu),
                                waits=(t_gmm, psum_free.get(("sg", esl2))), sig="ga")
                            psum_free[("g", gbn)] = t_sg
                            t_h1 = P.add("vector", lambda e, h1b=h1b, sgb=sgb, ub=ub: e.tensor_tensor(
                                out=h1b[:], in0=sgb[:], in1=ub, op=ALU.mult),
                                waits=(t_umm, t_sg, psum_free.get(("h1", esl2))), sig="gv")
                            psum_free[("u", gbn)] = t_h1
                            psum_free[("sg", esl2)] = t_h1
                            t_sact = P.add("vector", lambda e, h1b=h1b, ic=ic, tt=tt, wbt=t_wb[tt], esl2=esl2: e.tensor_tensor(
                                out=sb["sact"][:, TBLK * ic + TT * tt: TBLK * ic + TT * (tt + 1)],
                                in0=h1b[:], in1=sb[f"wb{tt}"][:], op=ALU.mult),
                                waits=(t_wb[tt], P._sact_war), sig="gv")
                            psum_free[("h1", esl2)] = t_sact
                            t_sact_last = t_sact
                    for o2 in range(16):
                        sl = o2 % 2
                        wdb = sb[f"wd{sl}"]
                        t_wd = P.add("sync", lambda e, wdb=wdb, ei=ei, o2=o2: e.dma_start(
                            out=wdb[:].rearrange("p (c o) -> p c o", c=NIC), in_=dTb[ei, o2]),
                            waits=(wslot_free.get(("d", sl)),), sig=f"wd{sl}", dma=True)
                        for tt in range(TBLK // TT):
                            psl = (o2 * 2 + tt) % 2
                            yb = bank[psl][:]
                            t_ymm = None
                            for c in range(NIC):
                                w = (t_wd, t_sact_last, psum_free.get(("y", psl)),
                                     bank_free.get("ln_sum") if psl == 0 else bank_free.get("ln_sq")) if c == 0 else ()
                                t_ymm = P.add("tensor", lambda e, c=c, yb=yb, wdb=wdb, tt=tt: e.matmul(
                                    yb, wdb[:, 128 * c:128 * (c + 1)],
                                    sb["sact"][:, TBLK * c + TT * tt: TBLK * c + TT * (tt + 1)],
                                    start=(c == 0), stop=(c == NIC - 1)),
                                    waits=w, sig="ymm" if c == NIC - 1 else None)
                            if tt == 1:
                                wslot_free[("d", sl)] = t_ymm
                            t_dn_last[("wb", tt)] = t_ymm
                            if ei == 0:
                                t_ya = P.add("vector", lambda e, yb=yb, o2=o2, tt=tt: e.tensor_copy(
                                    out=sb["yac"][:, TBLK * o2 + TT * tt: TBLK * o2 + TT * (tt + 1)], in_=yb),
                                    waits=(t_ymm, P._yac_war), sig="yev")
                            else:
                                t_ya = P.add("vector", lambda e, yb=yb, o2=o2, tt=tt: e.tensor_tensor(
                                    out=sb["yac"][:, TBLK * o2 + TT * tt: TBLK * o2 + TT * (tt + 1)],
                                    in0=sb["yac"][:, TBLK * o2 + TT * tt: TBLK * o2 + TT * (tt + 1)],
                                    in1=yb, op=ALU.add),
                                    waits=(t_ymm,), sig="yev")
                            psum_free[("y", psl)] = t_ya
                    P._sact_war = t_ymm
                bank_free["ln_sum"] = psum_free[("y", 0)]
                bank_free["ln_sq"] = psum_free[("y", 1)]
                t_zb = {}
                for tt in range(TBLK // TT):
                    zbk = bank[6 + tt][:]
                    t_zm = P.add("tensor", lambda e, tt=tt, zbk=zbk: e.matmul(
                        zbk, sb["esel"][:, 128 * 8:128 * 9],
                        sb["w9T"][0:9, TT * tt:TT * (tt + 1)],
                        start=True, stop=True),
                        waits=(t_w9, psum_free.get(("wb", tt))), sig="mmg")
                    t_zb[tt] = t_zm
                t_last_fo = {}
                for o2 in range(16):
                    for tt in range(TBLK // TT):
                        it2 = o2 * 2 + tt
                        osl = it2 % 2
                        xrb, fob, ftb = sb[f"xr{osl}"], sb[f"fo{osl}"], sb[f"ft{osl}"]
                        src = x_in[seq].rearrange("(c p) t -> p c t", p=128)[:, o2, t0 + TT * tt: t0 + TT * (tt + 1)]
                        t_xr = P.add("sync", lambda e, xrb=xrb, src=src: e.dma_start(out=xrb[:], in_=src),
                                     waits=(bank_free.get(f"xr{osl}"),), sig=f"xr{osl}", dma=True)
                        P.add("vector", lambda e, ftb=ftb, o2=o2, tt=tt, zbk=bank[6 + tt]: e.tensor_tensor(
                            out=ftb[:], in0=sb["xn"][:, TBLK * o2 + TT * tt: TBLK * o2 + TT * (tt + 1)],
                            in1=zbk[:], op=ALU.mult),
                            waits=(t_zb[tt], bank_free.get(f"ft{osl}")))
                        P.add("vector", lambda e, ftb=ftb, o2=o2, tt=tt: e.tensor_tensor(
                            out=ftb[:], in0=ftb[:],
                            in1=sb["yac"][:, TBLK * o2 + TT * tt: TBLK * o2 + TT * (tt + 1)], op=ALU.add))
                        t_fo = P.add("vector", lambda e, fob=fob, ftb=ftb, xrb=xrb: e.tensor_tensor(
                            out=fob[:], in0=ftb[:], in1=xrb[:], op=ALU.add),
                            waits=(t_xr, bank_free.get(f"fo{osl}")), sig="fov")
                        bank_free[f"ft{osl}"] = t_fo
                        bank_free[f"xr{osl}"] = t_fo
                        dst = x_out[seq].rearrange("(c p) t -> p c t", p=128)[:, o2, t0 + TT * tt: t0 + TT * (tt + 1)]
                        t_st = P.add("sync", lambda e, fob=fob, dst=dst: e.dma_start(out=dst, in_=fob[:]),
                                     waits=(t_fo,), sig="st", dma=True)
                        bank_free[f"fo{osl}"] = t_st
                        t_last_fo[tt] = t_fo
                P._xn_free = (t_umm, t_last_fo.get(0), t_last_fo.get(1))
                P._yac_war = (t_last_fo.get(0), t_last_fo.get(1))
                P._lnw["ln"] = (t_last_fo.get(0), t_last_fo.get(1))
                psum_free[("wb", 0)] = t_last_fo.get(0)
                psum_free[("wb", 1)] = t_last_fo.get(1)
                bank_free["ln_mb"] = psum_free.get(("g", 0))
                bank_free["ln_rb"] = psum_free.get(("g", 1))
                psum_free[("gu", 0)] = psum_free.get(("u", 0))
                P._w9war = t_sact_last
        P.add("sync", lambda e: e.nop(), waits=(Tok("st", P.counts["st"], "none"),))
        P.emit()
    return nc


def moe_inputs(x_fm, deq, w, c):
    return dict(x_in=x_fm, gT_bf=deq["gT_bf"], uT_bf=deq["uT_bf"], dT_bf=deq["dT_bf"],
                clsT_p=w["clsT_p"], ebias=w["ebias"], esel=c["esel_bf"], ident=c["ident_bf"],
                ones_col=c["ones_col_bf"], ones1=c["ones1_f32"])


def kernel(**inputs):
    n_cores = 8
    w = prep_weights(inputs)
    c = prep_consts()
    x = np.asarray(inputs["L0_attn_hidden_states"])
    xs = [np.ascontiguousarray(x[2 * cc:2 * cc + 2].transpose(0, 2, 1)) for cc in range(n_cores)]
    cores = list(range(n_cores))

    nc_d = build_dequant()
    di = dequant_inputs(w, c)
    deq = run_bass_kernel_spmd(nc_d, [di] * n_cores, core_ids=cores).results

    aws = []
    for cc in range(n_cores):
        wa = dict(w)
        for p, k in zip(("q", "k", "v", "o", "q1", "k1", "q2", "k2", "v1", "o1"),
                        ("wqT", "wkT", "wvT", "woT", "q1T", "k1T", "q2T", "k2T", "vT", "oT")):
            wa[k] = np.asarray(deq[cc][f"awb_{p}"])
        aws.append(wa)

    nc_f = build_ffn0()
    nc_a0 = build_attn(0)
    nc_m = build_moe()
    nc_a1 = build_attn(1)

    cur = xs
    def launch(nc_l, mk_in):
        nonlocal cur
        maps = [mk_in(cc, cur[cc]) for cc in range(n_cores)]
        res = run_bass_kernel_spmd(nc_l, maps, core_ids=cores).results
        cur = [np.asarray(r["x_out"]) for r in res]

    launch(nc_f, lambda cc, xc: ffn0_inputs(xc, deq[cc], c))
    launch(nc_a0, lambda cc, xc: attn_inputs(0, xc, aws[cc], c))
    launch(nc_f, lambda cc, xc: ffn0_inputs(xc, deq[cc], c))
    launch(nc_m, lambda cc, xc: moe_inputs(xc, deq[cc], w, c))
    launch(nc_a1, lambda cc, xc: attn_inputs(1, xc, aws[cc], c))
    launch(nc_m, lambda cc, xc: moe_inputs(xc, deq[cc], w, c))

    out = np.empty((16, S, D), dtype=np.float32)
    for cc in range(n_cores):
        out[2 * cc:2 * cc + 2] = cur[cc].transpose(0, 2, 1)
    return out
